# version 19
# baseline (speedup 1.0000x reference)
import numpy as np
import ml_dtypes
import concourse.bass as bass
import concourse.mybir as mybir
from concourse.bass_utils import run_bass_kernel_spmd

F32 = mybir.dt.float32
BF16 = mybir.dt.bfloat16
I8 = mybir.dt.int8
U32 = mybir.dt.uint32
AF = mybir.ActivationFunctionType
ALU = mybir.AluOpType
AX = mybir.AxisListType

T = 4096
D = 2048
H = 8
HC = H * 128
C = 64
NCH = T // C
TBLK = 512
NTB = T // TBLK
MYT = 2048
FBLK = 512
SCALE = 128.0 ** -0.5
LNS = float(np.log(SCALE))
EPS = 1e-5
ALL_PHASES = ("proj", "scan", "wo", "ffn")
BF = ml_dtypes.bfloat16


def _barrier(blk, S, tag):
    s = S("phb_" + tag)
    for name in ("sync", "gpsimd", "tensor", "vector", "scalar"):
        def _inc(e):
            s.inc(e.drain())
        getattr(blk, name)(_inc)
    for name in ("sync", "gpsimd", "tensor", "vector", "scalar"):
        def _wait(e):
            s.wait(e)
        getattr(blk, name)(_wait)


def _mkhop(S):
    def hv(eng, ins, name):
        s = S(name)
        s.inc(ins)
        s.wait(eng)
    return hv


class Sem:
    def __init__(self, nc, name):
        self.h = nc.alloc_semaphore(name)
        self.n = 0

    def inc(self, instr, k=1):
        instr.then_inc(self.h, k)
        self.n += k
        return instr

    def wait(self, eng):
        if self.n:
            eng.wait_ge(self.h, self.n)

    def wait_at(self, eng, n):
        if n > 0:
            eng.wait_ge(self.h, n)


def _arrw(w):
    d, cols = w.shape
    assert d == D
    return np.ascontiguousarray(
        w.reshape(16, 128, cols).transpose(1, 0, 2).reshape(128, 16 * cols))


def core_inputs(ins, c):
    b, hg = c // 2, c % 2
    hs = slice(hg * HC, (hg + 1) * HC)
    pslc = slice((1 - hg) * HC, (2 - hg) * HC)
    f32 = lambda k: np.ascontiguousarray(np.asarray(ins[k], np.float32))
    h = f32("L0_attn_hidden_states")[b]
    wo = f32("L0_attn_Wo")
    conv_arr = lambda a: np.ascontiguousarray(
        a.reshape(H, 128, 4).transpose(1, 0, 2).reshape(128, 4 * H))
    cv = f32("L0_attn_cv")[hs]
    wg8 = np.asarray(ins["L0_ffn_w_gate"])
    wu8 = np.asarray(ins["L0_ffn_w_up"])
    wd8 = np.asarray(ins["L0_ffn_w_down"])
    return {
        "h": np.ascontiguousarray(h),
        "h_res": np.ascontiguousarray(h[hg * MYT:(hg + 1) * MYT]),
        "wq": _arrw(f32("L0_attn_Wq")[:, hs]),
        "wk": _arrw(f32("L0_attn_Wk")[:, hs]),
        "wv": _arrw(f32("L0_attn_Wv")[:, hs]),
        "ww": _arrw(f32("L0_attn_Ww")[:, hs]),
        "cq": conv_arr(f32("L0_attn_cq")[hs]),
        "ck": conv_arr(f32("L0_attn_ck")[hs]),
        "cv": conv_arr(cv),
        "cw": conv_arr(cv),
        "wf1": _arrw(f32("L0_attn_Wf1")),
        "wf2": np.ascontiguousarray(f32("L0_attn_Wf2")[:, hs]),
        "wb": _arrw(f32("L0_attn_Wb")[:, hg * H:(hg + 1) * H]),
        "wg1": _arrw(f32("L0_attn_Wg1")),
        "wg2": np.ascontiguousarray(f32("L0_attn_Wg2")[:, hs]),
        "bg2": np.ascontiguousarray(f32("L0_attn_bg2")[hs].reshape(H, 128).T),
        "normw": np.ascontiguousarray(f32("L0_attn_norm_w").reshape(128, 1)),
        "wo": _arrw(np.ascontiguousarray(
            np.concatenate([wo[hs], wo[pslc]], axis=0))),
        "wgate": np.ascontiguousarray(
            wg8.reshape(16, 128, 64, 128).transpose(2, 1, 0, 3).reshape(64, 128, D)),
        "wup": np.ascontiguousarray(
            wu8.reshape(16, 128, 64, 128).transpose(2, 1, 0, 3).reshape(64, 128, D)),
        "wdown": np.ascontiguousarray(wd8.reshape(64, 128, D)),
        "sgate": np.ascontiguousarray(f32("L0_ffn_s_gate").reshape(64, 128).T),
        "sup": np.ascontiguousarray(f32("L0_ffn_s_up").reshape(64, 128).T),
        "sdown": np.ascontiguousarray(f32("L0_ffn_s_down").reshape(1, D)),
        "parity": np.array([[hg]], np.uint32),
    }


def build(phases=ALL_PHASES, debug=False, nch=NCH, do_barrier=True):
    nc = bass.Bass("TRN2", target_bir_lowering=False)
    EI, EO = "ExternalInput", "ExternalOutput"
    din = lambda name, shape, dt=F32: nc.dram_tensor(
        name, shape, dt, kind=EI).ap()

    I = {}
    I["h"] = din("h", [T, D])
    I["h_res"] = din("h_res", [MYT, D])
    for w in ("wq", "wk", "wv", "ww"):
        I[w] = din(w, [128, 16 * HC])
    for w in ("cq", "ck", "cv", "cw"):
        I[w] = din(w, [128, 4 * H])
    I["wf1"] = din("wf1", [128, 16 * 128])
    I["wf2"] = din("wf2", [128, HC])
    I["wb"] = din("wb", [128, 16 * H])
    I["wg1"] = din("wg1", [128, 16 * 128])
    I["wg2"] = din("wg2", [128, HC])
    I["bg2"] = din("bg2", [128, H])
    I["normw"] = din("normw", [128, 1])
    I["wo"] = din("wo", [128, 16 * D])
    I["wgate"] = din("wgate", [64, 128, D], I8)
    I["wup"] = din("wup", [64, 128, D], I8)
    I["wdown"] = din("wdown", [64, 128, D], I8)
    I["sgate"] = din("sgate", [128, 64])
    I["sup"] = din("sup", [128, 64])
    I["sdown"] = din("sdown", [1, D])
    I["parity"] = din("parity", [1, 1], U32)
    if "proj" not in phases and "scan" in phases:
        I["beta_in"] = din("beta_in", [128, 32 * H])
    out_t = nc.dram_tensor("out", [MYT, D], F32, kind=EO).ap()

    dbg = EO if debug else "Internal"
    h16 = nc.dram_tensor("h16", [T, D], BF16, kind="Internal").ap()
    spl = {x: nc.dram_tensor(f"sp_{x}", [HC, T], BF16, kind=dbg).ap()
           for x in ("q", "k", "v", "w", "g", "gt")}
    x1_32 = nc.dram_tensor("x1_32", [MYT, D], F32, kind=dbg).ap()
    x1_16 = nc.dram_tensor("x1_16", [MYT, D], BF16, kind="Internal").ap()
    xchg = nc.dram_tensor("xchg", [2, 128, H * T], BF16,
                          addr_space="Shared").ap()
    bar_a = nc.dram_tensor("bar_a", [1, 8], F32, kind="Internal").ap()
    bar_b = nc.dram_tensor("bar_b", [1, 8], F32, kind="Internal").ap()
    dbg_o = (nc.dram_tensor("dbg_o", [128, H * T], BF16, kind=EO).ap()
             if debug else None)

    tu = np.triu(np.ones((C, C), np.float32))
    consts = {
        "tub": nc.inline_tensor(np.tile(tu.astype(BF), (2, 1)), "c_tub").ap(),
        "tumb": nc.inline_tensor(
            np.tile(tu.astype(np.float32).reshape(C, 1, C), (2, 1, 1)),
            "c_tumb").ap(),
        "idb": nc.inline_tensor(np.tile(np.eye(C, dtype=BF), (2, 1)),
                                "c_idb").ap(),
        "onec": nc.inline_tensor(np.ones((128, 1), BF), "c_onec").ap(),
        "oner": nc.inline_tensor(np.ones((1, 128), BF), "c_oner").ap(),
    }

    sems = {}

    def S(name):
        if name not in sems:
            sems[name] = Sem(nc, name)
        return sems[name]

    from contextlib import ExitStack
    stk = ExitStack()
    sb = lambda name, shape, dt: stk.enter_context(
        nc.sbuf_tensor(name, shape, dt))

    P = {}
    P["beta_tm"] = sb("beta_tm", [128, 32 * H], F32)
    P["tub"] = sb("tub", [128, C], BF16)
    P["tumb"] = sb("tumb", [128, C], F32)
    P["idb"] = sb("idb", [128, C], BF16)
    P["onec"] = sb("onec", [128, 1], BF16)
    P["oner"] = sb("oner", [1, 128], BF16)
    P["normw"] = sb("normw_b", [128, 1], F32)
    P["cc"] = sb("cc", [128, 8], F32)

    with nc.Block() as blk:
        def p0(sy):
            ld = S("c_ld")
            ld.inc(sy.dma_start(out=P["tub"][:, :], in_=consts["tub"]), 16)
            ld.inc(sy.dma_start(
                out=P["tumb"][:, :],
                in_=consts["tumb"].rearrange("p o c -> p (o c)")), 16)
            ld.inc(sy.dma_start(out=P["idb"][:, :], in_=consts["idb"]), 16)
            ld.inc(sy.dma_start(out=P["onec"][:, :], in_=consts["onec"]), 16)
            ld.inc(sy.dma_start(out=P["oner"][:, :], in_=consts["oner"]), 16)
            ld.inc(sy.dma_start(out=P["normw"][:, :], in_=I["normw"]), 16)
            if "beta_in" in I:
                ld.inc(sy.dma_start(out=P["beta_tm"][:, :], in_=I["beta_in"]), 16)
            ld.wait(sy)
        blk.sync(p0)

        def p0v(ve):
            vals = [LNS, 1e-6, EPS, -0.5, 1.0 / 128.0, -1.0 / D, 1.0 / D,
                    1.0 / SCALE]
            for i, v in enumerate(vals):
                ins = ve.memset(P["cc"][:, i:i + 1], v)
            S("c_cc").inc(ins)
        blk.vector(p0v)

        if "proj" in phases:
            with nc.named_scope("proj"):
                _proj(nc, blk, S, I, P, h16, spl)
        if "scan" in phases:
            with nc.named_scope("scan"):
                _scan(nc, blk, S, I, P, spl, xchg, bar_a, bar_b, dbg_o, nch,
                      do_barrier)
        if "wo" in phases:
            with nc.named_scope("wo"):
                _wo(nc, blk, S, I, P, xchg, x1_32, x1_16,
                    prefetched=("scan" in phases))
        if "ffn" in phases:
            with nc.named_scope("ffn"):
                _ffn(nc, blk, S, I, P, x1_32, x1_16, out_t)

    stk.close()
    return nc


def _proj(nc, blk, S, I, P, h16, spl):
    hop = _mkhop(S)
    from contextlib import ExitStack
    ps = ExitStack()
    sb = lambda name, shape, dt: ps.enter_context(
        nc.sbuf_tensor(name, shape, dt))
    pst = lambda name, shape, dt: ps.enter_context(
        nc.psum_tensor(name, shape, dt))

    hstg = [sb(f"hstg{i}", [128, D], BF16) for i in range(2)]
    hfm = [sb(f"hfm{i}", [128, 16 * TBLK], BF16) for i in range(3)]
    wbuf = [sb(f"wbuf{i}", [128, 16 * HC], BF16) for i in range(2)]
    w128 = sb("w128", [128, 16 * 128], BF16)
    w128b = sb("w128b", [128, 16 * 128], BF16)
    wsm = sb("wsm", [128, HC], BF16)
    wsm2 = sb("wsm2", [128, HC], BF16)
    wbsb = sb("wbsb", [128, 16 * H], BF16)
    bg2b = sb("bg2b", [128, H], F32)
    cvb = {x: sb(f"cv_{x}", [128, 4 * H], F32) for x in "qkvw"}
    xa = [sb(f"xa{cb}", [128, 3 + TBLK], BF16) for cb in range(H)]
    xb = [sb(f"xb{cb}", [128, 3 + TBLK], BF16) for cb in range(H)]
    accb = [sb(f"accb{i}", [128, TBLK], BF16) for i in range(4)]
    spb = [sb(f"spb{i}", [128, TBLK], BF16) for i in range(6)]
    f1b = sb("f1b", [128, TBLK], BF16)
    g1b = sb("g1b", [128, TBLK], BF16)
    ebuf = sb("ebuf", [128, TBLK], F32)
    psP = [pst(f"psP{i}", [128, TBLK], F32) for i in range(4)]
    psF = pst("psF", [128, TBLK], F32)
    psX2 = pst("psX2", [128, TBLK], F32)
    psB = pst("psB", [128, H], F32)

    s_hc = [S("p_hc0"), S("p_hc1")]
    s_hw = [S("p_hw0"), S("p_hw1")]
    s_hf = [S("p_hf0"), S("p_hf1"), S("p_hf2")]
    s_wl = S("p_wl")
    s_mm = S("p_mm")
    s_cv = S("p_cv")
    s_si = S("p_si")
    s_sp = [S(f"p_sp{i}") for i in range(6)]
    s_te = S("p_te")
    s_ms = S("p_ms")
    s_f1m = S("p_f1m")
    s_f1e = S("p_f1e")
    s_hm = S("p_hm")
    s_bt = S("p_bt")

    def p_h16(gp):
        for tt in range(32):
            if tt >= 2:
                s_hw[tt % 2].wait_at(gp, 16 * (tt // 2))
            s_hc[tt % 2].inc(gp.dma_start(
                out=hstg[tt % 2][:, :],
                in_=I["h"][tt * 128:(tt + 1) * 128, :]), 16)
    blk.gpsimd(p_h16)

    def p_h16w(sy):
        for tt in range(32):
            s_hc[tt % 2].wait_at(sy, 16 * (tt // 2 + 1))
            s_hw[tt % 2].inc(sy.dma_start(
                out=h16[tt * 128:(tt + 1) * 128, :],
                in_=hstg[tt % 2][:, :]), 16)
    blk.sync(p_h16w)

    def p_wl(gp):
        s_wl.inc(gp.dma_start(out=wbuf[0][:, :], in_=I["wq"]), 16)
        s_wl.inc(gp.dma_start(out=wbuf[1][:, :], in_=I["wk"]), 16)
        s_wl.inc(gp.dma_start(out=wbsb[:, :], in_=I["wb"]), 16)
        s_wl.inc(gp.dma_start(out=wsm[:, :], in_=I["wf2"]), 16)
        s_wl.inc(gp.dma_start(out=wsm2[:, :], in_=I["wg2"]), 16)
        s_wl.inc(gp.dma_start(out=w128[:, :], in_=I["wf1"]), 16)
        s_wl.inc(gp.dma_start(out=w128b[:, :], in_=I["wg1"]), 16)
    blk.gpsimd(p_wl)

    def p_sl(sy):
        ld = S("c_ld")
        ld.inc(sy.dma_start(out=bg2b[:, :], in_=I["bg2"]), 16)
        for x in "qkvw":
            ld.inc(sy.dma_start(out=cvb[x][:, :], in_=I["c" + x]), 16)
    blk.sync(p_sl)

    def p_memset(ve):
        for buf in xa + xb:
            ins = ve.memset(buf[:, 0:3], 0.0)
        s_ms.inc(ins)
    blk.vector(p_memset)

    it = [0]
    spn = [0]
    se_batch = []

    def flush_se():
        if se_batch:
            def sec(se, items=tuple(se_batch)):
                for f in items:
                    f(se)
            blk.scalar(sec)
            se_batch.clear()

    def emit_qkvw(X, wt, xrow, cb, tb, par):
        pp = it[0] % 4
        n = it[0]

        def mm(te):
            for s in s_hf:
                s.wait(te)
            s_wl.wait(te)
            s_ms.wait(te)
            if n >= 4:
                s_cv.wait_at(te, n - 3)
            for dc in range(16):
                ins = te.matmul(psP[pp][:, :],
                                wt[:, dc * HC + cb * 128:dc * HC + (cb + 1) * 128],
                                hfm[par][:, dc * TBLK:(dc + 1) * TBLK],
                                start=(dc == 0), stop=(dc == 15))
            s_mm.inc(ins)
        blk.tensor(mm)

        def ev(ve):
            s_mm.wait(ve)
            if n >= 4:
                s_si.wait_at(ve, spn[0] - 3)
            x = xrow[cb]
            if tb > 0:
                hop(ve, ve.tensor_copy(x[:, 0:3], x[:, TBLK:TBLK + 3]),
                    "hopV")
            hop(ve, ve.tensor_copy(x[:, 3:3 + TBLK], psP[pp][:, :]), "hopV")
            cc = cvb[X]
            hop(ve, ve.tensor_scalar_mul(accb[pp][:, :], x[:, 3:3 + TBLK],
                                         cc[:, cb * 4 + 3:cb * 4 + 4]), "hopV")
            for i in (2, 1, 0):
                ins = ve.scalar_tensor_tensor(
                    accb[pp][:, :], x[:, i:i + TBLK],
                    cc[:, cb * 4 + i:cb * 4 + i + 1],
                    accb[pp][:, :], ALU.mult, ALU.add)
                if i > 0:
                    hop(ve, ins, "hopV")
            s_cv.inc(ins)
        blk.vector(ev)

        slot = spn[0] % 6
        ns = spn[0]
        ncv = s_cv.n
        nsp = s_sp[slot].n

        def si(se, slot=slot, pp=pp, ncv=ncv, nsp=nsp):
            s_cv.wait_at(se, ncv)
            s_sp[slot].wait_at(se, nsp)
            s_si.inc(se.activation(spb[slot][:, :], accb[pp][:, :], AF.Silu))
        se_batch.append(si)

        def spdma(sy, ns=ns, slot=slot):
            s_si.wait_at(sy, ns + 1)
            s_sp[slot].inc(sy.dma_start(
                out=spl[X][cb * 128:(cb + 1) * 128, tb * TBLK:(tb + 1) * TBLK],
                in_=spb[slot][:, :]), 16)
        blk.sync(spdma)
        it[0] += 1
        spn[0] += 1

    def emit_fgate(which, wa, wsmall, dstname, tb, par):
        def mm1(te):
            for s in s_hf:
                s.wait(te)
            s_wl.wait(te)
            if s_f1e.n:
                s_f1e.wait(te)
            for dc in range(16):
                ins = te.matmul(psF[:, :], wa[:, dc * 128:(dc + 1) * 128],
                                hfm[par][:, dc * TBLK:(dc + 1) * TBLK],
                                start=(dc == 0), stop=(dc == 15))
            s_f1m.inc(ins)
        blk.tensor(mm1)

        buf = f1b if which == "f" else g1b

        def ev1(ve):
            s_f1m.wait(ve)
            s_hm.wait(ve)
            s_f1e.inc(ve.tensor_copy(buf[:, :], psF[:, :]))
        blk.vector(ev1)

        for h in range(H):
            slot = spn[0] % 6
            ns = spn[0]
            nsp = s_sp[slot].n

            def mm2(te, h=h, ns=ns, slot=slot, nsp=nsp):
                s_f1e.wait(te)
                s_sp[slot].wait_at(te, nsp)
                s_si.wait_at(te, ns)
                ins = te.matmul(psX2[:, :], wsmall[:, h * 128:(h + 1) * 128],
                                buf[:, :], start=True, stop=True)
                s_hm.inc(ins)
            blk.tensor(mm2)
            nhm = s_hm.n

            def act2(se, h=h, slot=slot, which=which, nhm=nhm):
                s_hm.wait_at(se, nhm)
                if which == "f":
                    hop = S("sec")
                    hop.inc(se.activation(ebuf[:, :], psX2[:, :], AF.Exp,
                                          scale=-1.0))
                    hop.wait(se)
                    s_si.inc(se.activation(spb[slot][:, :], ebuf[:, :],
                                           AF.Ln, bias=1.0))
                else:
                    s_si.inc(se.activation(spb[slot][:, :], psX2[:, :],
                                           AF.Sigmoid, bias=bg2b[:, h:h + 1]))
            se_batch.append(act2)

            def spdma(sy, h=h, slot=slot, ns=ns, tb=tb, dstname=dstname):
                s_si.wait_at(sy, ns + 1)
                s_sp[slot].inc(sy.dma_start(
                    out=spl[dstname][h * 128:(h + 1) * 128,
                                     tb * TBLK:(tb + 1) * TBLK],
                    in_=spb[slot][:, :]), 16)
            blk.sync(spdma)
            spn[0] += 1

    gbt = [0]

    def emit_beta(tb, par):
        for tt in range(4):
            def mmb(te, tt=tt, nbt=gbt[0]):
                for s in s_hf:
                    s.wait(te)
                s_bt.wait_at(te, nbt)
                for dc in range(16):
                    ins = te.matmul(
                        psB[:, :],
                        hfm[par][:, dc * TBLK + tt * 128:dc * TBLK + (tt + 1) * 128],
                        wbsb[:, dc * H:(dc + 1) * H],
                        start=(dc == 0), stop=(dc == 15))
                s_mm.inc(ins)
            blk.tensor(mmb)
            nmmb = s_mm.n

            def actb(se, tt=tt, tb=tb, nmmb=nmmb):
                s_mm.wait_at(se, nmmb)
                s_bt.inc(se.activation(
                    P["beta_tm"][:, (tb * 4 + tt) * H:(tb * 4 + tt + 1) * H],
                    psB[:, :], AF.Sigmoid))
            se_batch.append(actb)
            gbt[0] += 1

    for pass_no in (1, 2):
        if pass_no == 2:
            def reload(gp):
                s_te.wait(gp)
                s_wl.inc(gp.dma_start(out=wbuf[0][:, :], in_=I["wv"]), 16)
                s_wl.inc(gp.dma_start(out=wbuf[1][:, :], in_=I["ww"]), 16)
            blk.gpsimd(reload)

            def remset(ve):
                for s in s_sp:
                    s.wait(ve)
                for buf in xa + xb:
                    ins = ve.memset(buf[:, 0:3], 0.0)
                s_ms.inc(ins)
            blk.vector(remset)

        tensors = ([("q", wbuf[0], xa), ("k", wbuf[1], xb)] if pass_no == 1
                   else [("v", wbuf[0], xa), ("w", wbuf[1], xb)])

        for tb in range(NTB):
            par = (tb + (0 if pass_no == 1 else NTB)) % 3

            def tr(sy, tb=tb, par=par):
                s_hw[0].wait_at(sy, 16 * min(16, 2 * tb + 2))
                s_hw[1].wait_at(sy, 16 * min(16, 2 * tb + 2))
                gtb = tb + (0 if pass_no == 1 else NTB)
                need = s_te.n - 2 if gtb >= 3 else 0
                s_te.wait_at(sy, need)
                for j in range(16):
                    s_hf[par].inc(sy.dma_start(
                        out=hfm[par][:, j * TBLK:(j + 1) * TBLK],
                        in_=h16[tb * TBLK:(tb + 1) * TBLK, j * 128:(j + 1) * 128],
                        transpose=True), 16)
            blk.sync(tr)

            for X, wt, xrow in tensors:
                for cb in range(H):
                    emit_qkvw(X, wt, xrow, cb, tb, par)
            flush_se()

            if pass_no == 2:
                emit_fgate("f", w128, wsm, "g", tb, par)
                emit_fgate("g1", w128b, wsm2, "gt", tb, par)
                emit_beta(tb, par)
                flush_se()

            def tedone(te):
                s_mm.wait(te)
                s_hm.wait(te)
                s_te.inc(te.nop())
            blk.tensor(tedone)

    def join(sy):
        for s in s_sp:
            s.wait(sy)
    blk.sync(join)

    def join2(se):
        s_bt.wait(se)
    blk.scalar(join2)
    _barrier(blk, S, "proj")
    ps.close()


def _scan(nc, blk, S, I, P, spl, xchg, bar_a, bar_b, dbg_o=None, nch=NCH,
          do_barrier=True):
    hop = _mkhop(S)
    from contextlib import ExitStack
    ps = ExitStack()
    sb = lambda name, shape, dt: ps.enter_context(
        nc.sbuf_tensor(name, shape, dt))
    pst = lambda name, shape, dt: ps.enter_context(
        nc.psum_tensor(name, shape, dt))

    qf = [sb(f"qf{i}", [128, H * TBLK], BF16) for i in range(2)]
    kf = [sb(f"kf{i}", [128, H * TBLK], BF16) for i in range(2)]
    gtf = [sb(f"gtf{i}", [128, H * TBLK], BF16) for i in range(2)]
    gfm = [sb(f"gfm{i}", [128, H * 128], BF16) for i in range(2)]
    w2t_ = [sb(f"w2t{i}", [128, HC], BF16) for i in range(2)]
    sst_ = [sb(f"sst{i}", [128, H], F32) for i in range(2)]
    gs0_ = [sb(f"gs0{i}", [128, H], F32) for i in range(2)]
    gs1_ = [sb(f"gs1{i}", [128, H], F32) for i in range(2)]
    rsqt = [sb(f"rsqt{i}", [128, H], F32) for i in range(2)]
    brst = [sb(f"brst{i}", [128, H], F32) for i in range(2)]
    lcE = [sb(f"lcE{i}", [128, 2 * H], F32) for i in range(2)]
    ktm = [sb(f"ktm{i}", [128, HC], BF16) for i in range(2)]
    vtm = [sb(f"vtm{i}", [128, HC], BF16) for i in range(2)]
    wtm = [sb(f"wtm{i}", [128, HC], BF16) for i in range(2)]
    gtm = [sb(f"gtm{i}", [128, HC], BF16) for i in range(2)]
    Sk = [sb(f"Sk{i}", [128, HC], BF16) for i in range(2)]
    Sv = [sb(f"Sv{i}", [128, HC], BF16) for i in range(2)]
    ostg = [sb(f"ostg{i}", [128, H * TBLK], BF16) for i in range(2)]

    d2 = lambda name, shape, dt=BF16: [sb(name + str(i), shape, dt)
                                       for i in range(2)]
    Eb = d2("Eb", [128, HC]); Ab = d2("Ab", [128, HC])
    Ub = d2("Ub", [128, HC]); w2b = d2("w2b", [128, HC])
    kqs = d2("kqs", [128, H * C]); Xs = d2("Xs", [128, HC])
    bcs = d2("bcs", [128, HC]); sbf = d2("sbf", [128, HC])
    s2b = d2("s2b", [128, HC]); eeb = d2("eeb", [128, HC])
    Pb = d2("Pb", [128, HC]); KQm = d2("KQm", [128, H * C])
    w2s = d2("w2s", [128, H * C]); W2Tm = d2("W2Tm", [128, H * C])
    UPT = d2("UPT", [128, 2 * H * C]); O2b = d2("O2b", [128, H * C])
    Osb = d2("Osb", [128, H * C]); rbs = d2("rbs", [128, H * C])
    Oab = d2("Oab", [128, H * C])
    ssb = d2("ssb", [128, H], F32); t8a = d2("t8a", [128, H], F32)
    rsqb = d2("rsqb", [128, H], F32); brsq = d2("brsq", [128, H], F32)
    mxb = d2("mxb", [128, H], F32); smb = d2("smb", [128, H], F32)
    rrb = d2("rrb", [128, H], F32); lcol = d2("lcol", [128, H], F32)
    lrow = d2("lrow", [1, HC], BF16)
    rfr = d2("rfr", [1, H * C], F32); rfr2 = d2("rfr2", [1, H * C], BF16)

    PS1 = pst("PS1", [128, HC], F32)
    PS3 = pst("PS3", [128, H * C], F32)
    PS4 = pst("PS4", [128, H * C], F32)
    PS5 = pst("PS5", [128, 2 * H * C], BF16)
    PS6 = pst("PS6", [128, 512], F32)

    s_fm = [S("s_fm0"), S("s_fm1")]
    s_tm = [S("s_tm0"), S("s_tm1")]
    s_ost = [S("s_ost0"), S("s_ost1")]
    s_bar = S("s_bar")

    A_ = {}
    for nm in ("a1", "a2", "a3", "a4", "a5", "a6", "a7", "a8", "a9", "a10",
               "a11", "a12", "a13", "a14", "a15", "a16", "a17", "a18", "a19",
               "a20", "a21", "a22", "a23", "a25", "a26", "a28", "a4",
               "b1", "b2", "b3", "b4"):
        A_[nm] = S("sc_" + nm)

    tub, tumb, idb, onec, oner = (P["tub"], P["tumb"], P["idb"], P["onec"],
                                  P["oner"])
    beta_tm = P["beta_tm"]
    v3 = lambda t: t[:, :].rearrange("p (hh x) -> p hh x", hh=H)

    def init(ve):
        S("c_ld").wait(ve)
        A_["a28"].inc(ve.memset(Sk[1][:, :], 0.0))
        A_["a25"].inc(ve.memset(Sv[1][:, :], 0.0))
    blk.vector(init)
    ST0 = 1

    def init_te(te):
        S("c_ld").wait(te)
    blk.tensor(init_te)

    fm_cnt = [0, 0]
    tm_cnt = [0, 0]
    fm_ld = fm_cnt
    tm_ld = tm_cnt

    def chunk_done_at(eng, n):
        A_["a22"].wait_at(eng, n)
        A_["a26"].wait_at(eng, n)
        A_["a28"].wait_at(eng, ST0 + n)

    def load_fm(blkid):
        def f(sy, blkid=blkid):
            if blkid >= 2:
                chunk_done_at(sy, 8 * (blkid - 1))
            bp = blkid % 2
            for name, buf in (("q", qf), ("k", kf), ("gt", gtf)):
                srcv = spl[name].rearrange("(hh p) t -> p hh t", p=128)
                s_fm[bp].inc(sy.dma_start(
                    out=buf[bp][:, :].rearrange("p (hh t) -> p hh t", hh=H),
                    in_=srcv[:, :, blkid * TBLK:(blkid + 1) * TBLK]), 16)
            fm_cnt[bp] = s_fm[bp].n
        blk.sync(f)

    def load_tm(tt):
        def f(sy, tt=tt):
            if tt >= 2:
                chunk_done_at(sy, 2 * (tt - 1))
            tp = tt % 2
            for name, buf in (("k", ktm), ("v", vtm), ("w", wtm), ("g", gtm)):
                s_tm[tp].inc(sy.dma_start(
                    out=buf[tp][:, :],
                    in_=spl[name][:, tt * 128:(tt + 1) * 128],
                    transpose=True), 16)
            srcg = spl["g"].rearrange("(hh p) t -> p hh t", p=128)
            s_tm[tp].inc(sy.dma_start(
                out=gfm[tp][:, :].rearrange("p (hh t) -> p hh t", hh=H),
                in_=srcg[:, :, tt * 128:(tt + 1) * 128]), 16)
            tm_cnt[tp] = s_tm[tp].n
        blk.sync(f)

    load_fm(0)
    load_fm(1)
    load_tm(0)
    load_tm(1)

    te_as, te_bs, te_eps = [], [], []
    se_as, se_bs, se_eps = [], [], []
    ve_as, ve_bs, ve_eps = [], [], []

    def w_at(eng, nm, n):
        A_[nm].wait_at(eng, n)

    def te_s1(te, c, R, tp, bp, tib, fmc, tmc):
        g3 = gtm[tp]
        ro = C if R.start == 0 else 0
        s_fm[bp].wait_at(te, fmc)
        s_tm[tp].wait_at(te, tmc)
        w_at(te, "a25", ST0 + c)
        w_at(te, "a7", c)
        w_at(te, "a2", c)
        te.matmul(PS1[R, 0:512], tub[R, :], g3[R, 0:512],
                  start=True, stop=True)
        te.matmul(PS1[R, 512:1024], tub[R, :], g3[R, 512:1024],
                  start=True, stop=True)
        te.matmul(PS1[ro:ro + 1, 0:512], onec[R, :], g3[R, 0:512],
                  start=True, stop=True)
        te.matmul(PS1[ro:ro + 1, 512:1024], onec[R, :], g3[R, 512:1024],
                  start=True, stop=True)
        for h in range(H):
            ins = te.matmul(PS3[R, h * C:(h + 1) * C],
                            kf[bp][:, h * TBLK + tib:h * TBLK + tib + C],
                            qf[bp][:, h * TBLK + tib:h * TBLK + tib + C],
                            start=True, stop=True)
        A_["a1"].inc(ins)

    for cix in range(nch):
        tt = cix // 2
        beta = cix % 2
        blkid = cix // 8
        bp = blkid % 2
        tp = tt % 2
        cp = cix % 2
        tib = (cix * C) % TBLK
        R = slice(beta * C, beta * C + C)
        cur, nxt = (cix + 1) % 2, cix % 2
        c = cix
        fmc = fm_cnt[bp]
        tmc = tm_cnt[tp]
        if cix + 1 < nch:
            fmc2 = fm_cnt[((cix + 1) // 8) % 2]
            tmc2 = tm_cnt[((cix + 1) // 2) % 2]
        else:
            fmc2 = tmc2 = 0
        ostn = s_ost[(cix // 8) % 2].n if cix // 8 >= 2 else 0

        if cix == 0:
            def te_pro(te, fmc=fmc, tmc=tmc):
                te_s1(te, 0, slice(0, C), 0, 0, 0, fmc, tmc)
            blk.tensor(te_pro)

        def te_a(te, c=c, R=R, tp=tp, bp=bp, tib=tib, cur=cur, cp=cp):
            w_at(te, "a5", c + 1)
            w_at(te, "a12", c)
            for h in range(H):
                te.matmul(PS1[R, h * 128:(h + 1) * 128],
                          KQm[cp][R, h * C:(h + 1) * C],
                          Ub[cp][R, h * 128:(h + 1) * 128],
                          start=True, stop=False)
                te.matmul(PS1[R, h * 128:(h + 1) * 128],
                          qf[bp][:, h * TBLK + tib:h * TBLK + tib + C],
                          Sk[cur][:, h * 128:(h + 1) * 128],
                          start=False, stop=True)
            for h in range(H):
                ins = te.transpose(PS5[:, h * C:(h + 1) * C],
                                   Ub[cp][R, h * 128:(h + 1) * 128], idb[R, :])
            A_["a6"].inc(ins)

        def te_b(te, c=c, R=R, tp=tp, bp=bp, tib=tib, cur=cur, cp=cp,
                 fmc2=fmc2, tmc2=tmc2):
            w_at(te, "a2", c + 1)
            w_at(te, "a17", c)
            te.matmul(PS3[:, 0:512], oner[0:1, :], lrow[cp][0:1, 0:512],
                      start=True, stop=True)
            ins = te.matmul(PS4[:, 0:512], oner[0:1, :],
                            lrow[cp][0:1, 512:1024], start=True, stop=True)
            A_["a4"].inc(ins)
            w_at(te, "a7", c + 1)
            w_at(te, "a21", c)
            for h in range(H):
                te.matmul(PS1[:, h * 128:(h + 1) * 128],
                          ktm[tp][R, h * 128:(h + 1) * 128],
                          Ub[cp][R, h * 128:(h + 1) * 128],
                          start=True, stop=True)
            for h in range(4):
                ins = te.matmul(PS6[:, h * 128:(h + 1) * 128],
                                Ub[cp][R, h * 128:(h + 1) * 128],
                                vtm[tp][R, h * 128:(h + 1) * 128],
                                start=True, stop=True)
            A_["a23"].inc(ins)
            w_at(te, "a10", c + 1)
            for h in range(H):
                ins = te.transpose(PS5[:, (H + h) * C:(H + h + 1) * C],
                                   Pb[cp][R, h * 128:(h + 1) * 128], idb[R, :])
            A_["a11"].inc(ins)
            w_at(te, "a12", c + 1)
            w_at(te, "a25", ST0 + c + 1)
            w_at(te, "a21", c)
            for h in range(H):
                ins = te.matmul(PS6[R, h * 64:(h + 1) * 64],
                                UPT[cp][:, h * C:(h + 1) * C],
                                UPT[cp][:, (H + h) * C:(H + h + 1) * C],
                                start=True, stop=True)
            A_["a13"].inc(ins)
            w_at(te, "a15", c + 1)
            w_at(te, "a28", ST0 + c)
            w_at(te, "a17", c)
            for h in range(H):
                te.matmul(PS4[:, h * C:(h + 1) * C],
                          Sv[cur][:, h * 128:(h + 1) * 128],
                          UPT[cp][:, (H + h) * C:(H + h + 1) * C],
                          start=True, stop=False)
                ins = te.matmul(PS4[:, h * C:(h + 1) * C],
                                vtm[tp][R, h * 128:(h + 1) * 128],
                                W2Tm[cp][R, h * C:(h + 1) * C],
                                start=False, stop=True)
            A_["a16"].inc(ins)
            if c + 1 < nch:
                c2 = c + 1
                te_s1(te, c2, slice((c2 % 2) * C, (c2 % 2) * C + C),
                      (c2 // 2) % 2, (c2 // 8) % 2, (c2 * C) % TBLK,
                      fmc2, tmc2)
            w_at(te, "a25", ST0 + c + 1)
            w_at(te, "a14", c + 1)
            for h in range(4, 8):
                ins = te.matmul(PS6[:, (h - 4) * 128:(h - 3) * 128],
                                Ub[cp][R, h * 128:(h + 1) * 128],
                                vtm[tp][R, h * 128:(h + 1) * 128],
                                start=True, stop=True)
            A_["a26"].inc(ins)

        def te_ep(te, c=c, cp=cp):
            w_at(te, "a17", c + 1)
            w_at(te, "a28", ST0 + c + 1)
            for h in range(H):
                ins = te.matmul(PS6[0:1, h * 64:(h + 1) * 64], onec[:, :],
                                O2b[cp][:, h * C:(h + 1) * C],
                                start=True, stop=True)
            A_["a18"].inc(ins)
            w_at(te, "a19", c + 1)
            ins = te.matmul(PS6[:, :], oner[0:1, :], rfr2[cp][0:1, :],
                            start=True, stop=True)
            A_["a20"].inc(ins)
        te_as.append(te_a)
        te_bs.append(te_b)
        te_eps.append(te_ep)

        def se_a(se, c=c, R=R, tp=tp, cp=cp, tt=tt, beta=beta):
            S("c_cc").wait(se)
            if beta == 0:
                w_at(se, "a1", c + 1)
                w_at(se, "b2", tt - 1)
                ins = se.activation(w2t_[tp][:, :], wtm[tp][:, :], AF.Square)
                A_["b1"].inc(ins)
            w_at(se, "a1", c + 1)
            w_at(se, "a5", c - 1)
            w_at(se, "a10", c - 1)
            w_at(se, "a6", c - 1)
            w_at(se, "a3", c - 1)
            ro = C if beta == 0 else 0
            se.activation(Eb[cp][R, :], PS1[R, :], AF.Exp)
            se.activation(Ab[cp][R, :], PS1[R, :], AF.Exp, scale=-1.0,
                          bias=P["cc"][R, 0:1])
            se.activation(lrow[cp][0:1, :], PS1[ro:ro + 1, :], AF.Exp,
                          scale=-1.0)
            ins = se.activation(kqs[cp][R, :], PS3[R, :], AF.Copy)
            A_["a2"].inc(ins)
            if beta == 0:
                w_at(se, "b2", tt + 1)
                w_at(se, "b4", tt - 1)
                w_at(se, "a28", ST0 + 2 * tt - 2)
                hop(se, se.activation(sst_[tp][:, :], sst_[tp][:, :], AF.Ln,
                                      bias=P["cc"][:, 1:2]), "hopS")
                se.activation(rsqt[tp][:, :], sst_[tp][:, :], AF.Exp,
                              scale=P["cc"][:, 3:4])
                se.activation(lcE[tp][:, 0:H], gs0_[tp][:, :], AF.Exp,
                              scale=-1.0)
                ins = se.activation(lcE[tp][:, H:2 * H], gs1_[tp][:, :],
                                    AF.Exp, scale=-1.0)
                A_["b3"].inc(ins)

        def se_b(se, c=c, R=R, tp=tp, cp=cp, tt=tt, beta=beta):
            w_at(se, "a6", c + 1)
            w_at(se, "a4", c + 1)
            w_at(se, "a8", c - 1)
            w_at(se, "a25", ST0 + c)
            se.activation(Xs[cp][R, :], PS1[R, :], AF.Copy)
            se.activation(bcs[cp][:, 0:512], PS3[:, 0:512], AF.Copy)
            ins = se.activation(bcs[cp][:, 512:1024], PS4[:, 0:512], AF.Copy)
            A_["a7"].inc(ins)
            w_at(se, "a8", c + 1)
            w_at(se, "a10", c - 1)
            ins = se.activation(eeb[cp][R, :], s2b[cp][R, :], AF.Exp)
            A_["a9"].inc(ins)
            w_at(se, "a13", c + 1)
            w_at(se, "a15", c - 1)
            ins = se.activation(w2s[cp][R, :], PS6[R, 0:512], AF.Copy)
            A_["a14"].inc(ins)
            w_at(se, "a16", c + 1)
            w_at(se, "a18", c - 1)
            w_at(se, "a22", c - 1)
            se.activation(O2b[cp][:, :], PS4[:, :], AF.Square)
            ins = se.activation(Osb[cp][:, :], PS4[:, :], AF.Copy)
            A_["a17"].inc(ins)

        def se_ep(se, c=c, cp=cp):
            w_at(se, "a18", c + 1)
            w_at(se, "a20", c - 1)
            hop(se, se.activation(rfr[cp][0:1, :], PS6[0:1, :], AF.Ln,
                                  scale=P["cc"][0:1, 4:5],
                                  bias=P["cc"][0:1, 2:3]), "hopS")
            ins = se.activation(rfr2[cp][0:1, :], rfr[cp][0:1, :], AF.Exp,
                                scale=P["cc"][0:1, 3:4])
            A_["a19"].inc(ins)
            w_at(se, "a20", c + 1)
            w_at(se, "a22", c - 1)
            ins = se.activation(rbs[cp][:, :], PS6[:, :], AF.Copy)
            A_["a21"].inc(ins)
        se_as.append(se_a)
        se_bs.append(se_b)
        se_eps.append(se_ep)

        def ve_a(ve, c=c, R=R, tp=tp, cp=cp, tt=tt, beta=beta):
            if beta == 0:
                w_at(ve, "b1", tt + 1)
                w_at(ve, "b3", tt - 1)
                ve.reduce_sum(sst_[tp][:, :], v3(w2t_[tp])[:, :, :], axis=AX.X)
                gv = gfm[tp][:, :].rearrange("p (hh t) -> p hh t", hh=H)
                ve.reduce_sum(gs0_[tp][:, :], gv[:, :, 0:C], axis=AX.X)
                ins = ve.reduce_sum(gs1_[tp][:, :], gv[:, :, C:2 * C],
                                    axis=AX.X)
                A_["b2"].inc(ins)
            w_at(ve, "a2", c + 1)
            w_at(ve, "a6", c - 1)
            ins = ve.tensor_tensor(
                v3(KQm[cp])[R, :, :],
                kqs[cp][R, :].rearrange("p (hh x) -> p hh x", hh=H),
                tumb[R, :].rearrange("p (o x) -> p o x", o=1).broadcast_to(
                    [C, H, C]), ALU.mult)
            A_["a3"].inc(ins)
            if beta == 0:
                w_at(ve, "b3", tt + 1)
                w_at(ve, "a5", 2 * tt - 2)
                ins = ve.tensor_mul(brst[tp][:, :],
                                    beta_tm[:, tt * H:(tt + 1) * H],
                                    rsqt[tp][:, :])
                A_["b4"].inc(ins)
            w_at(ve, "b4", tt + 1)
            w_at(ve, "a26", c - 1)
            hop(ve, ve.tensor_mul(Ub[cp][R, :], Eb[cp][R, :], wtm[tp][R, :]),
                "hopV")
            ins = ve.tensor_tensor(
                v3(Ub[cp])[R, :, :], v3(Ub[cp])[R, :, :],
                brst[tp][R, :].rearrange("p (hh o) -> p hh o",
                                         o=1).broadcast_to([C, H, 128]),
                ALU.mult)
            A_["a5"].inc(ins)

        def ve_b(ve, c=c, R=R, tp=tp, cur=cur, nxt=nxt, cp=cp, tt=tt,
                 beta=beta):
            w_at(ve, "a7", c + 1)
            w_at(ve, "a9", c - 1)
            hop(ve, ve.tensor_mul(sbf[cp][R, :], Xs[cp][R, :], Ab[cp][R, :]),
                "hopV")
            hop(ve, ve.reduce_max(mxb[cp][R, :], v3(sbf[cp])[R, :, :],
                                  axis=AX.X), "hopV")
            ins = ve.tensor_tensor(
                v3(s2b[cp])[R, :, :], v3(sbf[cp])[R, :, :],
                mxb[cp][R, :].rearrange("p (hh o) -> p hh o",
                                        o=1).broadcast_to([C, H, 128]),
                ALU.subtract)
            A_["a8"].inc(ins)
            w_at(ve, "a9", c + 1)
            w_at(ve, "a11", c - 1)
            ve.reduce_sum(smb[cp][R, :], v3(eeb[cp])[R, :, :], axis=AX.X)
            hop(ve, ve.tensor_mul(Pb[cp][R, :], eeb[cp][R, :], Ab[cp][R, :]),
                "hopV")
            hop(ve, ve.reciprocal(rrb[cp][R, :], smb[cp][R, :]), "hopV")
            hop(ve, ve.tensor_scalar_mul(rrb[cp][R, :], rrb[cp][R, :],
                                         P["cc"][R, 7:8]), "hopV")
            ins = ve.tensor_tensor(
                v3(Pb[cp])[R, :, :], v3(Pb[cp])[R, :, :],
                rrb[cp][R, :].rearrange("p (hh o) -> p hh o",
                                        o=1).broadcast_to([C, H, 128]),
                ALU.mult)
            A_["a10"].inc(ins)
            w_at(ve, "a11", c + 1)
            w_at(ve, "a16", c - 1)
            ins = ve.tensor_copy(UPT[cp][:, :], PS5[:, :])
            A_["a12"].inc(ins)
            w_at(ve, "a23", c + 1)
            w_at(ve, "a6", c)
            hop(ve, ve.tensor_tensor(Sk[nxt][:, :], Sk[cur][:, :], PS1[:, :],
                                     ALU.add), "hopV")
            hop(ve, ve.tensor_mul(Sk[nxt][:, :], Sk[nxt][:, :],
                                  bcs[cp][:, :]), "hopV")
            ins = ve.tensor_tensor(Sv[nxt][:, 0:512], Sv[cur][:, 0:512],
                                   PS6[:, :], ALU.add)
            A_["a25"].inc(ins)
            w_at(ve, "a14", c + 1)
            w_at(ve, "a16", c - 1)
            ins = ve.tensor_tensor(
                v3(W2Tm[cp])[R, :, :],
                w2s[cp][R, :].rearrange("p (hh x) -> p hh x", hh=H),
                tumb[R, :].rearrange("p (o x) -> p o x", o=1).broadcast_to(
                    [C, H, C]), ALU.mult)
            A_["a15"].inc(ins)
            w_at(ve, "a26", c + 1)
            w_at(ve, "a16", c)
            w_at(ve, "b3", tt + 1)
            hop(ve, ve.tensor_tensor(Sv[nxt][:, 512:1024],
                                     Sv[cur][:, 512:1024], PS6[:, :],
                                     ALU.add), "hopV")
            ins = ve.tensor_tensor(
                v3(Sv[nxt])[:, :, :], v3(Sv[nxt])[:, :, :],
                lcE[tp][:, beta * H:(beta + 1) * H].rearrange(
                    "p (hh o) -> p hh o", o=1).broadcast_to([128, H, 128]),
                ALU.mult)
            A_["a28"].inc(ins)

        def ve_ep(ve, c=c, cp=cp, cix=cix, tib=tib, bp=bp, ostn=ostn):
            w_at(ve, "a21", c + 1)
            w_at(ve, "a17", c + 1)
            if cix // 8 >= 2:
                s_ost[(cix // 8) % 2].wait_at(ve, ostn)
            hop(ve, ve.tensor_mul(Oab[cp][:, :], Osb[cp][:, :],
                                  rbs[cp][:, :]), "hopV")
            os = (cix // 8) % 2
            ins = ve.tensor_tensor(
                v3(ostg[os])[:, :, tib:tib + C],
                Oab[cp][:, :].rearrange("p (hh x) -> p hh x", hh=H),
                v3(gtf[bp])[:, :, tib:tib + C], ALU.mult)
            A_["a22"].inc(ins)
        ve_as.append(ve_a)
        ve_bs.append(ve_b)
        ve_eps.append(ve_ep)

        if cix % 8 == 7:
            def oflush(sy, cix=cix):
                A_["a22"].wait_at(sy, cix + 1)
                os = (cix // 8) % 2
                blkid2 = cix // 8
                xv = [xchg[i, :, :].rearrange("p (hh t) -> p hh t", hh=H)
                      for i in range(2)]
                with sy.register(f"par_s{cix}") as reg:
                    sy.reg_load(reg, I["parity"][0:1, 0:1])
                    with sy.If_eq(reg, 0):
                        s_ost[os].inc(sy.dma_start(
                            out=xv[0][:, :, blkid2 * TBLK:(blkid2 + 1) * TBLK],
                            in_=v3(ostg[os])[:, :, :]), 16)
                    with sy.Else():
                        sy.dma_start(
                            out=xv[1][:, :, blkid2 * TBLK:(blkid2 + 1) * TBLK],
                            in_=v3(ostg[os])[:, :, :]).then_inc(
                                s_ost[os].h, 16)
                if dbg_o is not None:
                    dv = dbg_o.rearrange("p (hh t) -> p hh t", hh=H)
                    s_ost[os].inc(sy.dma_start(
                        out=dv[:, :, blkid2 * TBLK:(blkid2 + 1) * TBLK],
                        in_=v3(ostg[os])[:, :, :]), 16)
            blk.sync(oflush)

        if cix % 2 == 1 and tt + 2 < 32:
            load_tm(tt + 2)
        if cix % 8 == 7 and blkid + 2 < NTB:
            load_fm(blkid + 2)

    GRP = 16
    for g0 in range(0, nch, GRP):
        cs = list(range(g0, min(g0 + GRP, nch)))

        def te_sec(te, cs=cs):
            for c2 in cs:
                te_as[c2](te)
                if c2 >= 1:
                    te_eps[c2 - 1](te)
                te_bs[c2](te)
        blk.tensor(te_sec)

        def se_sec(se, cs=cs):
            for c2 in cs:
                se_as[c2](se)
                if c2 >= 1:
                    se_eps[c2 - 1](se)
                se_bs[c2](se)
        blk.scalar(se_sec)

        def ve_sec(ve, cs=cs):
            for c2 in cs:
                ve_as[c2](ve)
                if c2 >= 1:
                    ve_eps[c2 - 1](ve)
                ve_bs[c2](ve)
        blk.vector(ve_sec)

    def te_tail(te):
        te_eps[nch - 1](te)
    blk.tensor(te_tail)

    def se_tail(se):
        se_eps[nch - 1](se)
    blk.scalar(se_tail)

    def ve_tail(ve):
        ve_eps[nch - 1](ve)
    blk.vector(ve_tail)

    def sjoin(sy):
        s_ost[0].wait(sy)
        s_ost[1].wait(sy)
    blk.sync(sjoin)

    if do_barrier:
        def barrier(gp):
            s_ost[0].wait(gp)
            s_ost[1].wait(gp)
            s_bar.inc(gp.collective_compute(
                "AllReduce", ALU.add, replica_groups=[list(range(8))],
                ins=[bar_a], outs=[bar_b]))
            s_bar.wait(gp)
        blk.gpsimd(barrier)
    _barrier(blk, S, "scan")
    ps.close()


def _wo(nc, blk, S, I, P, xchg, x1_32, x1_16, prefetched=False):
    hop = _mkhop(S)
    from contextlib import ExitStack
    ps = ExitStack()
    sb = lambda name, shape, dt: ps.enter_context(
        nc.sbuf_tensor(name, shape, dt))
    pst = lambda name, shape, dt: ps.enter_context(
        nc.psum_tensor(name, shape, dt))

    wo_sb = sb("wo_sb", [128, 16 * D], BF16)
    om = [sb(f"om{i}", [128, HC], BF16) for i in range(2)]
    opb = [sb(f"opb{i}", [128, HC], BF16) for i in range(2)]
    hres = [sb(f"hres{i}", [128, D], F32) for i in range(2)]
    z1 = sb("z1", [128, D], F32)
    x1f = sb("x1f", [128, D], F32)
    x1b = sb("x1b", [128, D], BF16)
    msum = sb("msum", [128, 1], F32)
    nmean = sb("nmean", [128, 1], F32)
    ssq = sb("ssq", [128, 1], F32)
    em2 = sb("em2", [128, 1], F32)
    vtmp = sb("vtmp", [128, 1], F32)
    rstd = sb("rstd", [128, 1], F32)
    sqs = sb("sqs", [128, D], BF16)
    PW = [pst(f"PW{i}", [128, D], F32) for i in range(2)]

    s_wg = S("w_wg")
    s_wsc = S("w_wsc")
    s_ot = [S("w_ot0"), S("w_ot1")]
    s_hr = [S("w_hr0"), S("w_hr1")]
    s_wmm = S("w_mm")
    s_wev = S("w_ev")
    s_lnA = S("w_lnA")
    s_ln2 = S("w_ln2")
    s_lnB = S("w_lnB")
    s_ln4 = S("w_ln4")
    s_x1d = S("w_x1d")

    def wol(gp):
        s_wg.inc(gp.dma_start(out=wo_sb[:, :], in_=I["wo"]), 16)
    blk.gpsimd(wol)

    def wosc(ve):
        S("c_ld").wait(ve)
        s_wg.wait(ve)
        s_wsc.inc(ve.tensor_scalar_mul(wo_sb[:, :], wo_sb[:, :],
                                       P["normw"][:, 0:1]))
    blk.vector(wosc)

    xv = [xchg[i, :, :].rearrange("p (hh t) -> p hh t", hh=H)
          for i in range(2)]
    wo_seb = []

    for i in range(16):
        pp = i % 2

        def otl(sy, i=i, pp=pp):
            if i >= 2:
                s_wmm.wait_at(sy, i - 1)
                s_wev.wait_at(sy, i - 1)
            with sy.register(f"par_w{i}") as reg:
                sy.reg_load(reg, I["parity"][0:1, 0:1])
                with sy.If_eq(reg, 0):
                    s_ot[pp].inc(sy.dma_start(
                        out=om[pp][:, :].rearrange("p (hh d) -> p hh d", hh=H),
                        in_=xv[0][:, :, i * 128:(i + 1) * 128]), 16)
                    s_ot[pp].inc(sy.dma_start(
                        out=opb[pp][:, :].rearrange("p (hh d) -> p hh d", hh=H),
                        in_=xv[1][:, :, i * 128:(i + 1) * 128]), 16)
                with sy.Else():
                    sy.dma_start(
                        out=om[pp][:, :].rearrange("p (hh d) -> p hh d", hh=H),
                        in_=xv[1][:, :, MYT + i * 128:MYT + (i + 1) * 128]
                    ).then_inc(s_ot[pp].h, 16)
                    sy.dma_start(
                        out=opb[pp][:, :].rearrange("p (hh d) -> p hh d", hh=H),
                        in_=xv[0][:, :, MYT + i * 128:MYT + (i + 1) * 128]
                    ).then_inc(s_ot[pp].h, 16)
            s_hr[pp].inc(sy.dma_start(
                out=hres[pp][:, :],
                in_=I["h_res"][i * 128:(i + 1) * 128, :]), 16)
        blk.sync(otl)

        def wmm(te, i=i, pp=pp):
            s_ot[pp].wait(te)
            s_wsc.wait(te)
            if i >= 2:
                s_wev.wait_at(te, i - 1)
            for cg in range(4):
                for hd in range(16):
                    src = (om[pp][:, (hd % H) * 128:(hd % H + 1) * 128]
                           if hd < H else
                           opb[pp][:, (hd - H) * 128:(hd - H + 1) * 128])
                    ins = te.matmul(
                        PW[pp][:, cg * 512:(cg + 1) * 512], src,
                        wo_sb[:, hd * D + cg * 512:hd * D + (cg + 1) * 512],
                        start=(hd == 0), stop=(hd == 15))
            s_wmm.inc(ins)
        blk.tensor(wmm)

        def wev(ve, i=i, pp=pp):
            s_wmm.wait(ve)
            s_hr[pp].wait(ve)
            if i >= 1:
                s_ln4.wait_at(ve, i)
            hop(ve, ve.tensor_add(z1[:, :], PW[pp][:, :], hres[pp][:, :]),
                "hopV")
            ins = ve.reduce_sum(msum[:, :],
                                z1[:, :].rearrange("p (o d) -> p o d", o=1),
                                axis=AX.X)
            s_wev.inc(ins)
        blk.vector(wev)

        def wln(se, i=i, nv=s_wev.n):
            S("c_cc").wait(se)
            s_wev.wait_at(se, nv)
            se.activation(sqs[:, :], z1[:, :], AF.Square, accum_out=ssq[:, :])
            hop = S("sec")
            hop.inc(se.activation(nmean[:, :], msum[:, :], AF.Copy,
                                  scale=P["cc"][:, 5:6]))
            hop.wait(se)
            s_lnA.inc(se.activation(em2[:, :], nmean[:, :], AF.Square))
        wo_seb.append(wln)

        def wln2(ve, i=i):
            s_lnA.wait_at(ve, i + 1)
            s_ln2.inc(ve.scalar_tensor_tensor(
                vtmp[:, :], ssq[:, :], P["cc"][:, 6:7], em2[:, :],
                ALU.mult, ALU.subtract))
        blk.vector(wln2)

        def wln3(se, i=i):
            s_ln2.wait_at(se, i + 1)
            hop = S("sec")
            hop.inc(se.activation(vtmp[:, :], vtmp[:, :], AF.Ln,
                                  bias=P["cc"][:, 2:3]))
            hop.wait(se)
            s_lnB.inc(se.activation(rstd[:, :], vtmp[:, :], AF.Exp,
                                    scale=P["cc"][:, 3:4]))
        wo_seb.append(wln3)

        def wln4(ve, i=i):
            s_lnB.wait_at(ve, i + 1)
            if i >= 1:
                s_x1d.wait_at(ve, 32 * i)
            hop(ve, ve.tensor_scalar(x1f[:, :], z1[:, :], nmean[:, 0:1],
                                     rstd[:, 0:1], ALU.add, ALU.mult), "hopV")
            s_ln4.inc(ve.tensor_copy(x1b[:, :], x1f[:, :]))
        blk.vector(wln4)

        def wst(sy, i=i):
            s_ln4.wait(sy)
            s_x1d.inc(sy.dma_start(out=x1_32[i * 128:(i + 1) * 128, :],
                                   in_=x1f[:, :]), 16)
            s_x1d.inc(sy.dma_start(out=x1_16[i * 128:(i + 1) * 128, :],
                                   in_=x1b[:, :]), 16)
        blk.sync(wst)

    def wo_sec(se, items=tuple(wo_seb)):
        for f in items:
            f(se)
    blk.scalar(wo_sec)

    def wjoin(sy):
        s_x1d.wait(sy)
    blk.sync(wjoin)
    _barrier(blk, S, "wo")
    ps.close()


def _ffn(nc, blk, S, I, P, x1_32, x1_16, out_t):
    hop = _mkhop(S)
    from contextlib import ExitStack
    ps = ExitStack()
    sb = lambda name, shape, dt: ps.enter_context(
        nc.sbuf_tensor(name, shape, dt))
    pst = lambda name, shape, dt: ps.enter_context(
        nc.psum_tensor(name, shape, dt))

    NB = MYT // FBLK
    xfm = sb("xfm", [128, 16 * FBLK], BF16)
    fuse = sb("fuse", [128, 64 * FBLK], BF16)
    wgb = [sb(f"wgb{i}", [128, 16 * 128], BF16) for i in range(2)]
    wub = [sb(f"wub{i}", [128, 16 * 128], BF16) for i in range(2)]
    wdb = [sb(f"wdb{i}", [128, 16 * 512], BF16) for i in range(2)]
    sgl = [sb(f"sgl{i}", [128, FBLK], BF16) for i in range(2)]
    sgsb = sb("sgsb", [128, 64], F32)
    susb = sb("susb", [128, 64], F32)
    sdro = sb("sdro", [1, D], BF16)
    sdbc = sb("sdbc", [128, D], BF16)
    z2 = [sb(f"z2_{i}", [128, D], F32) for i in range(4)]
    x1r = sb("x1r", [128, D], F32)
    x2b = sb("x2b", [128, D], F32)
    msum = sb("fmsum", [128, 1], F32)
    nmean = sb("fnmean", [128, 1], F32)
    ssq = sb("fssq", [128, 1], F32)
    em2 = sb("fem2", [128, 1], F32)
    vtmp = sb("fvtmp", [128, 1], F32)
    rstd = sb("frstd", [128, 1], F32)
    sqs = sb("fsqs", [128, D], BF16)
    PG = [pst(f"PG{i}", [128, FBLK], F32) for i in range(2)]
    PU = [pst(f"PU{i}", [128, FBLK], F32) for i in range(2)]
    PD = [pst(f"PD{i}", [128, 512], F32) for i in range(4)]
    PSD = PD[0]

    s_fl = S("f_fl")
    s_sd1 = S("f_sd1")
    s_sd2 = S("f_sd2")
    s_fw = S("f_fw")
    s_gu = [S("f_gu0"), S("f_gu1")]
    s_wd = [S("f_wd0"), S("f_wd1")]
    s_amm = S("f_amm")
    s_asi = S("f_asi")
    s_afu = S("f_afu")
    s_dmm = S("f_dmm")
    s_bev = S("f_bev")
    s_x1r = S("f_x1r")
    s_flna = S("f_flna")
    s_l2a = S("f_l2a")
    s_l2v = S("f_l2v")
    s_l2b = S("f_l2b")
    s_ln5 = S("f_ln5")
    s_out = S("f_out")

    def fsl(gp):
        s_fl.inc(gp.dma_start(out=sgsb[:, :], in_=I["sgate"]), 16)
        s_fl.inc(gp.dma_start(out=susb[:, :], in_=I["sup"]), 16)
        s_fl.inc(gp.dma_start(out=sdro[0:1, :], in_=I["sdown"]), 16)
    blk.gpsimd(fsl)

    for cg in range(4):
        def sdm(te, cg=cg):
            S("c_ld").wait(te)
            s_fl.wait(te)
            if cg > 0:
                s_sd2.wait(te)
            s_sd1.inc(te.matmul(PSD[:, :], P["oner"][0:1, :],
                                sdro[0:1, cg * 512:(cg + 1) * 512],
                                start=True, stop=True))
        blk.tensor(sdm)

        def sde(ve, cg=cg):
            s_sd1.wait(ve)
            s_sd2.inc(ve.tensor_copy(sdbc[:, cg * 512:(cg + 1) * 512],
                                     PSD[:, :]))
        blk.vector(sde)

    gic = [0]
    gicg = [0]
    fn_seb = []

    def fn_flush():
        if fn_seb:
            def sec(se, items=tuple(fn_seb)):
                for f in items:
                    f(se)
            blk.scalar(sec)
            fn_seb.clear()

    for b in range(NB):
        def xtr(sy, b=b):
            if b >= 1:
                s_amm.wait(sy)
            for j in range(16):
                s_fw.inc(sy.dma_start(
                    out=xfm[:, j * FBLK:(j + 1) * FBLK],
                    in_=x1_16[b * FBLK:(b + 1) * FBLK, j * 128:(j + 1) * 128],
                    transpose=True), 16)
        blk.sync(xtr)

        amm_base = S("f_amm").n
        asi_base = S("f_asi").n
        afu_base = S("f_afu").n
        gu_base = [s.n for s in s_gu]

        def wl_all(gp, b=b):
            for ic in range(64):
                wp = ic % 2
                n = gic[0] + ic
                if n >= 2:
                    s_amm.wait_at(gp, n - 1)
                s_gu[wp].inc(gp.dma_start(out=wgb[wp][:, :],
                                          in_=I["wgate"][ic, :, :]), 16)
                s_gu[wp].inc(gp.dma_start(out=wub[wp][:, :],
                                          in_=I["wup"][ic, :, :]), 16)
        blk.gpsimd(wl_all)

        def amm_all(te, b=b):
            for ic in range(64):
                wp = ic % 2
                n = gic[0] + ic
                s_gu[wp].wait_at(te, gu_base[wp] + 32 * (ic // 2 + 1))
                s_fw.wait(te)
                if n >= 2:
                    s_afu.wait_at(te, afu_base + ic - 1)
                s_dmm.wait(te)
                for dc in range(16):
                    te.matmul(PG[wp][:, :], wgb[wp][:, dc * 128:(dc + 1) * 128],
                              xfm[:, dc * FBLK:(dc + 1) * FBLK],
                              start=(dc == 0), stop=(dc == 15))
                for dc in range(16):
                    ins = te.matmul(PU[wp][:, :],
                                    wub[wp][:, dc * 128:(dc + 1) * 128],
                                    xfm[:, dc * FBLK:(dc + 1) * FBLK],
                                    start=(dc == 0), stop=(dc == 15))
                s_amm.inc(ins)
        blk.tensor(amm_all)

        def asi_all(se, b=b):
            for ic in range(64):
                wp = ic % 2
                s_amm.wait_at(se, amm_base + ic + 1)
                s_asi.inc(se.activation(sgl[wp][:, :], PG[wp][:, :], AF.Silu,
                                        scale=sgsb[:, ic:ic + 1]))
        blk.scalar(asi_all)

        def afu_all(ve, b=b):
            for ic in range(64):
                wp = ic % 2
                s_asi.wait_at(ve, asi_base + ic + 1)
                s_afu.inc(ve.scalar_tensor_tensor(
                    fuse[:, ic * FBLK:(ic + 1) * FBLK], PU[wp][:, :],
                    susb[:, ic:ic + 1], sgl[wp][:, :], ALU.mult, ALU.mult))
        blk.vector(afu_all)
        gic[0] += 64

        for cg in range(4):
            for icg in range(4):
                wp = gicg[0] % 2
                m = gicg[0]

                def wdl(gp, cg=cg, icg=icg, wp=wp, m=m):
                    if m >= 2:
                        s_dmm.wait_at(gp, m - 1)
                    s_wd[wp].inc(gp.dma_start(
                        out=wdb[wp][:, :].rearrange("p (a t) -> p a t", a=16),
                        in_=I["wdown"][icg * 16:(icg + 1) * 16, :,
                                       cg * 512:(cg + 1) * 512].rearrange(
                                           "a p t -> p a t")), 16)
                blk.gpsimd(wdl)

                def bmm(te, cg=cg, icg=icg, wp=wp):
                    s_wd[wp].wait(te)
                    s_afu.wait(te)
                    s_sd2.wait(te)
                    if icg == 0:
                        s_bev.wait(te)
                    for ic in range(16):
                        for sub in range(4):
                            ins = te.matmul(
                                PD[sub][:, :],
                                fuse[:, (icg * 16 + ic) * FBLK + sub * 128:
                                     (icg * 16 + ic) * FBLK + (sub + 1) * 128],
                                wdb[wp][:, ic * 512:(ic + 1) * 512],
                                start=(icg == 0 and ic == 0),
                                stop=(icg == 3 and ic == 15))
                    s_dmm.inc(ins)
                blk.tensor(bmm)
                gicg[0] += 1

            def bev(ve, b=b, cg=cg):
                s_dmm.wait(ve)
                s_sd2.wait(ve)
                if b > 0 or cg > 0:
                    s_ln5.wait(ve)
                for sub in range(4):
                    ins = ve.tensor_mul(z2[sub][:, cg * 512:(cg + 1) * 512],
                                        PD[sub][:, :],
                                        sdbc[:, cg * 512:(cg + 1) * 512])
                s_bev.inc(ins)
            blk.vector(bev)

        for sub in range(4):
            def x1l(sy, b=b, sub=sub):
                s_flna.wait(sy)
                s_x1r.inc(sy.dma_start(
                    out=x1r[:, :],
                    in_=x1_32[b * FBLK + sub * 128:b * FBLK + (sub + 1) * 128, :]),
                    16)
            blk.sync(x1l)

            def fln(ve, sub=sub):
                s_bev.wait(ve)
                s_x1r.wait(ve)
                hop(ve, ve.tensor_add(z2[sub][:, :], z2[sub][:, :],
                                      x1r[:, :]), "hopV")
                ins = ve.reduce_sum(
                    msum[:, :],
                    z2[sub][:, :].rearrange("p (o d) -> p o d", o=1),
                    axis=AX.X)
                s_flna.inc(ins)
            blk.vector(fln)

            jg = b * 4 + sub

            def fln2(se, sub=sub, nfl=s_flna.n):
                S("c_cc").wait(se)
                s_flna.wait_at(se, nfl)
                se.activation(sqs[:, :], z2[sub][:, :], AF.Square,
                              accum_out=ssq[:, :])
                hop = S("sec")
                hop.inc(se.activation(nmean[:, :], msum[:, :], AF.Copy,
                                      scale=P["cc"][:, 5:6]))
                hop.wait(se)
                s_l2a.inc(se.activation(em2[:, :], nmean[:, :], AF.Square))
            fn_seb.append(fln2)

            def fln3(ve, sub=sub, jg=jg):
                s_l2a.wait_at(ve, jg + 1)
                s_l2v.inc(ve.scalar_tensor_tensor(
                    vtmp[:, :], ssq[:, :], P["cc"][:, 6:7], em2[:, :],
                    ALU.mult, ALU.subtract))
            blk.vector(fln3)

            def fln4(se, sub=sub, jg=jg):
                s_l2v.wait_at(se, jg + 1)
                hop = S("sec")
                hop.inc(se.activation(vtmp[:, :], vtmp[:, :], AF.Ln,
                                      bias=P["cc"][:, 2:3]))
                hop.wait(se)
                s_l2b.inc(se.activation(rstd[:, :], vtmp[:, :], AF.Exp,
                                        scale=P["cc"][:, 3:4]))
            fn_seb.append(fln4)

            def fln5(ve, sub=sub, jg=jg):
                s_l2b.wait_at(ve, jg + 1)
                s_out.wait(ve)
                s_ln5.inc(ve.tensor_scalar(
                    x2b[:, :], z2[sub][:, :], nmean[:, 0:1], rstd[:, 0:1],
                    ALU.add, ALU.mult))
            blk.vector(fln5)

            def fst(sy, b=b, sub=sub):
                s_ln5.wait(sy)
                r0 = b * FBLK + sub * 128
                s_out.inc(sy.dma_start(out=out_t[r0:r0 + 128, :],
                                       in_=x2b[:, :]), 16)
            blk.sync(fst)
        fn_flush()

    def fin(sy):
        s_out.wait(sy)
    blk.sync(fin)
    ps.close()


def kernel(**inputs):
    nc = build()
    in_maps = [core_inputs(inputs, c) for c in range(8)]
    res = run_bass_kernel_spmd(nc, in_maps, core_ids=list(range(8)))
    full = np.empty((4, T, D), np.float32)
    for b in range(4):
        full[b, 0:MYT] = res.results[2 * b]["out"]
        full[b, MYT:T] = res.results[2 * b + 1]["out"]
    return full


# version 20
# speedup vs baseline: 1.0020x; 1.0020x over previous
import numpy as np
import ml_dtypes
import concourse.bass as bass
import concourse.mybir as mybir
from concourse.bass_utils import run_bass_kernel_spmd

F32 = mybir.dt.float32
BF16 = mybir.dt.bfloat16
I8 = mybir.dt.int8
U32 = mybir.dt.uint32
AF = mybir.ActivationFunctionType
ALU = mybir.AluOpType
AX = mybir.AxisListType

T = 4096
D = 2048
H = 8
HC = H * 128
C = 64
NCH = T // C
TBLK = 512
NTB = T // TBLK
MYT = 2048
FBLK = 512
SCALE = 128.0 ** -0.5
LNS = float(np.log(SCALE))
EPS = 1e-5
ALL_PHASES = ("proj", "scan", "wo", "ffn")
BF = ml_dtypes.bfloat16


def _barrier(blk, S, tag):
    s = S("phb_" + tag)
    for name in ("sync", "gpsimd", "tensor", "vector", "scalar"):
        def _inc(e):
            s.inc(e.drain())
        getattr(blk, name)(_inc)
    for name in ("sync", "gpsimd", "tensor", "vector", "scalar"):
        def _wait(e):
            s.wait(e)
        getattr(blk, name)(_wait)


def _mkhop(S):
    def hv(eng, ins, name):
        s = S(name)
        s.inc(ins)
        s.wait(eng)
    return hv


class Sem:
    def __init__(self, nc, name):
        self.h = nc.alloc_semaphore(name)
        self.n = 0

    def inc(self, instr, k=1):
        instr.then_inc(self.h, k)
        self.n += k
        return instr

    def wait(self, eng):
        if self.n:
            eng.wait_ge(self.h, self.n)

    def wait_at(self, eng, n):
        if n > 0:
            eng.wait_ge(self.h, n)


def _arrw(w):
    d, cols = w.shape
    assert d == D
    return np.ascontiguousarray(
        w.reshape(16, 128, cols).transpose(1, 0, 2).reshape(128, 16 * cols))


def core_inputs(ins, c):
    b, hg = c // 2, c % 2
    hs = slice(hg * HC, (hg + 1) * HC)
    pslc = slice((1 - hg) * HC, (2 - hg) * HC)
    f32 = lambda k: np.ascontiguousarray(np.asarray(ins[k], np.float32))
    h = f32("L0_attn_hidden_states")[b]
    wo = f32("L0_attn_Wo")
    conv_arr = lambda a: np.ascontiguousarray(
        a.reshape(H, 128, 4).transpose(1, 0, 2).reshape(128, 4 * H))
    cv = f32("L0_attn_cv")[hs]
    wg8 = np.asarray(ins["L0_ffn_w_gate"])
    wu8 = np.asarray(ins["L0_ffn_w_up"])
    wd8 = np.asarray(ins["L0_ffn_w_down"])
    return {
        "h": np.ascontiguousarray(h),
        "h_res": np.ascontiguousarray(h[hg * MYT:(hg + 1) * MYT]),
        "wq": _arrw(f32("L0_attn_Wq")[:, hs]),
        "wk": _arrw(f32("L0_attn_Wk")[:, hs]),
        "wv": _arrw(f32("L0_attn_Wv")[:, hs]),
        "ww": _arrw(f32("L0_attn_Ww")[:, hs]),
        "cq": conv_arr(f32("L0_attn_cq")[hs]),
        "ck": conv_arr(f32("L0_attn_ck")[hs]),
        "cv": conv_arr(cv),
        "cw": conv_arr(cv),
        "wf1": _arrw(f32("L0_attn_Wf1")),
        "wf2": np.ascontiguousarray(f32("L0_attn_Wf2")[:, hs]),
        "wb": _arrw(f32("L0_attn_Wb")[:, hg * H:(hg + 1) * H]),
        "wg1": _arrw(f32("L0_attn_Wg1")),
        "wg2": np.ascontiguousarray(f32("L0_attn_Wg2")[:, hs]),
        "bg2": np.ascontiguousarray(f32("L0_attn_bg2")[hs].reshape(H, 128).T),
        "normw": np.ascontiguousarray(f32("L0_attn_norm_w").reshape(128, 1)),
        "wo": _arrw(np.ascontiguousarray(
            np.concatenate([wo[hs], wo[pslc]], axis=0))),
        "wgate": np.ascontiguousarray(
            wg8.reshape(16, 128, 64, 128).transpose(2, 1, 0, 3).reshape(64, 128, D)),
        "wup": np.ascontiguousarray(
            wu8.reshape(16, 128, 64, 128).transpose(2, 1, 0, 3).reshape(64, 128, D)),
        "wdown": np.ascontiguousarray(wd8.reshape(64, 128, D)),
        "sgate": np.ascontiguousarray(f32("L0_ffn_s_gate").reshape(64, 128).T),
        "sup": np.ascontiguousarray(f32("L0_ffn_s_up").reshape(64, 128).T),
        "sdown": np.ascontiguousarray(f32("L0_ffn_s_down").reshape(1, D)),
        "parity": np.array([[hg]], np.uint32),
    }


def build(phases=ALL_PHASES, debug=False, nch=NCH, do_barrier=True):
    nc = bass.Bass("TRN2", target_bir_lowering=False)
    EI, EO = "ExternalInput", "ExternalOutput"
    din = lambda name, shape, dt=F32: nc.dram_tensor(
        name, shape, dt, kind=EI).ap()

    I = {}
    I["h"] = din("h", [T, D])
    I["h_res"] = din("h_res", [MYT, D])
    for w in ("wq", "wk", "wv", "ww"):
        I[w] = din(w, [128, 16 * HC])
    for w in ("cq", "ck", "cv", "cw"):
        I[w] = din(w, [128, 4 * H])
    I["wf1"] = din("wf1", [128, 16 * 128])
    I["wf2"] = din("wf2", [128, HC])
    I["wb"] = din("wb", [128, 16 * H])
    I["wg1"] = din("wg1", [128, 16 * 128])
    I["wg2"] = din("wg2", [128, HC])
    I["bg2"] = din("bg2", [128, H])
    I["normw"] = din("normw", [128, 1])
    I["wo"] = din("wo", [128, 16 * D])
    I["wgate"] = din("wgate", [64, 128, D], I8)
    I["wup"] = din("wup", [64, 128, D], I8)
    I["wdown"] = din("wdown", [64, 128, D], I8)
    I["sgate"] = din("sgate", [128, 64])
    I["sup"] = din("sup", [128, 64])
    I["sdown"] = din("sdown", [1, D])
    I["parity"] = din("parity", [1, 1], U32)
    if "proj" not in phases and "scan" in phases:
        I["beta_in"] = din("beta_in", [128, 32 * H])
    out_t = nc.dram_tensor("out", [MYT, D], F32, kind=EO).ap()

    dbg = EO if debug else "Internal"
    h16 = nc.dram_tensor("h16", [T, D], BF16, kind="Internal").ap()
    spl = {x: nc.dram_tensor(f"sp_{x}", [HC, T], BF16, kind=dbg).ap()
           for x in ("q", "k", "v", "w", "g", "gt")}
    x1_32 = nc.dram_tensor("x1_32", [MYT, D], F32, kind=dbg).ap()
    x1_16 = nc.dram_tensor("x1_16", [MYT, D], BF16, kind="Internal").ap()
    xchg = nc.dram_tensor("xchg", [2, 128, H * T], BF16,
                          addr_space="Shared").ap()
    bar_a = nc.dram_tensor("bar_a", [1, 8], F32, kind="Internal").ap()
    bar_b = nc.dram_tensor("bar_b", [1, 8], F32, kind="Internal").ap()
    dbg_o = (nc.dram_tensor("dbg_o", [128, H * T], BF16, kind=EO).ap()
             if debug else None)

    tu = np.triu(np.ones((C, C), np.float32))
    consts = {
        "tub": nc.inline_tensor(np.tile(tu.astype(BF), (2, 1)), "c_tub").ap(),
        "tumb": nc.inline_tensor(
            np.tile(tu.astype(np.float32).reshape(C, 1, C), (2, 1, 1)),
            "c_tumb").ap(),
        "idb": nc.inline_tensor(np.tile(np.eye(C, dtype=BF), (2, 1)),
                                "c_idb").ap(),
        "onec": nc.inline_tensor(np.ones((128, 1), BF), "c_onec").ap(),
        "oner": nc.inline_tensor(np.ones((1, 128), BF), "c_oner").ap(),
    }

    sems = {}

    def S(name):
        if name not in sems:
            sems[name] = Sem(nc, name)
        return sems[name]

    from contextlib import ExitStack
    stk = ExitStack()
    sb = lambda name, shape, dt: stk.enter_context(
        nc.sbuf_tensor(name, shape, dt))

    P = {}
    P["beta_tm"] = sb("beta_tm", [128, 32 * H], F32)
    P["tub"] = sb("tub", [128, C], BF16)
    P["tumb"] = sb("tumb", [128, C], F32)
    P["idb"] = sb("idb", [128, C], BF16)
    P["onec"] = sb("onec", [128, 1], BF16)
    P["oner"] = sb("oner", [1, 128], BF16)
    P["normw"] = sb("normw_b", [128, 1], F32)
    P["cc"] = sb("cc", [128, 8], F32)

    with nc.Block() as blk:
        def p0(sy):
            ld = S("c_ld")
            ld.inc(sy.dma_start(out=P["tub"][:, :], in_=consts["tub"]), 16)
            ld.inc(sy.dma_start(
                out=P["tumb"][:, :],
                in_=consts["tumb"].rearrange("p o c -> p (o c)")), 16)
            ld.inc(sy.dma_start(out=P["idb"][:, :], in_=consts["idb"]), 16)
            ld.inc(sy.dma_start(out=P["onec"][:, :], in_=consts["onec"]), 16)
            ld.inc(sy.dma_start(out=P["oner"][:, :], in_=consts["oner"]), 16)
            ld.inc(sy.dma_start(out=P["normw"][:, :], in_=I["normw"]), 16)
            if "beta_in" in I:
                ld.inc(sy.dma_start(out=P["beta_tm"][:, :], in_=I["beta_in"]), 16)
            ld.wait(sy)
        blk.sync(p0)

        def p0v(ve):
            vals = [LNS, 1e-6, EPS, -0.5, 1.0 / 128.0, -1.0 / D, 1.0 / D,
                    1.0 / SCALE]
            for i, v in enumerate(vals):
                ins = ve.memset(P["cc"][:, i:i + 1], v)
            S("c_cc").inc(ins)
        blk.vector(p0v)

        if "proj" in phases:
            with nc.named_scope("proj"):
                _proj(nc, blk, S, I, P, h16, spl)
        if "scan" in phases:
            with nc.named_scope("scan"):
                _scan(nc, blk, S, I, P, spl, xchg, bar_a, bar_b, dbg_o, nch,
                      do_barrier)
        if "wo" in phases:
            with nc.named_scope("wo"):
                _wo(nc, blk, S, I, P, xchg, x1_32, x1_16,
                    prefetched=("scan" in phases))
        if "ffn" in phases:
            with nc.named_scope("ffn"):
                _ffn(nc, blk, S, I, P, x1_32, x1_16, out_t)

    stk.close()
    return nc


def _proj(nc, blk, S, I, P, h16, spl):
    hop = _mkhop(S)
    from contextlib import ExitStack
    ps = ExitStack()
    sb = lambda name, shape, dt: ps.enter_context(
        nc.sbuf_tensor(name, shape, dt))
    pst = lambda name, shape, dt: ps.enter_context(
        nc.psum_tensor(name, shape, dt))

    hstg = [sb(f"hstg{i}", [128, D], BF16) for i in range(2)]
    hfm = [sb(f"hfm{i}", [128, 16 * TBLK], BF16) for i in range(3)]
    wbuf = [sb(f"wbuf{i}", [128, 16 * HC], BF16) for i in range(2)]
    w128 = sb("w128", [128, 16 * 128], BF16)
    w128b = sb("w128b", [128, 16 * 128], BF16)
    wsm = sb("wsm", [128, HC], BF16)
    wsm2 = sb("wsm2", [128, HC], BF16)
    wbsb = sb("wbsb", [128, 16 * H], BF16)
    bg2b = sb("bg2b", [128, H], F32)
    cvb = {x: sb(f"cv_{x}", [128, 4 * H], F32) for x in "qkvw"}
    xa = [sb(f"xa{cb}", [128, 3 + TBLK], BF16) for cb in range(H)]
    xb = [sb(f"xb{cb}", [128, 3 + TBLK], BF16) for cb in range(H)]
    accb = [sb(f"accb{i}", [128, TBLK], BF16) for i in range(4)]
    spb = [sb(f"spb{i}", [128, TBLK], BF16) for i in range(6)]
    f1b = sb("f1b", [128, TBLK], BF16)
    g1b = sb("g1b", [128, TBLK], BF16)
    ebuf = sb("ebuf", [128, TBLK], F32)
    psP = [pst(f"psP{i}", [128, TBLK], F32) for i in range(4)]
    psF = pst("psF", [128, TBLK], F32)
    psX2 = pst("psX2", [128, TBLK], F32)
    psB = pst("psB", [128, H], F32)

    s_hc = [S("p_hc0"), S("p_hc1")]
    s_hw = [S("p_hw0"), S("p_hw1")]
    s_hf = [S("p_hf0"), S("p_hf1"), S("p_hf2")]
    s_wl = S("p_wl")
    s_mm = S("p_mm")
    s_cv = S("p_cv")
    s_si = S("p_si")
    s_sp = [S(f"p_sp{i}") for i in range(6)]
    s_te = S("p_te")
    s_ms = S("p_ms")
    s_f1m = S("p_f1m")
    s_f1e = S("p_f1e")
    s_hm = S("p_hm")
    s_bt = S("p_bt")

    def p_h16(gp):
        for tt in range(32):
            if tt >= 2:
                s_hw[tt % 2].wait_at(gp, 16 * (tt // 2))
            s_hc[tt % 2].inc(gp.dma_start(
                out=hstg[tt % 2][:, :],
                in_=I["h"][tt * 128:(tt + 1) * 128, :]), 16)
    blk.gpsimd(p_h16)

    def p_h16w(sy):
        for tt in range(32):
            s_hc[tt % 2].wait_at(sy, 16 * (tt // 2 + 1))
            s_hw[tt % 2].inc(sy.dma_start(
                out=h16[tt * 128:(tt + 1) * 128, :],
                in_=hstg[tt % 2][:, :]), 16)
    blk.sync(p_h16w)

    def p_wl(gp):
        s_wl.inc(gp.dma_start(out=wbuf[0][:, :], in_=I["wq"]), 16)
        s_wl.inc(gp.dma_start(out=wbuf[1][:, :], in_=I["wk"]), 16)
        s_wl.inc(gp.dma_start(out=wbsb[:, :], in_=I["wb"]), 16)
        s_wl.inc(gp.dma_start(out=wsm[:, :], in_=I["wf2"]), 16)
        s_wl.inc(gp.dma_start(out=wsm2[:, :], in_=I["wg2"]), 16)
        s_wl.inc(gp.dma_start(out=w128[:, :], in_=I["wf1"]), 16)
        s_wl.inc(gp.dma_start(out=w128b[:, :], in_=I["wg1"]), 16)
    blk.gpsimd(p_wl)

    def p_sl(sy):
        ld = S("c_ld")
        ld.inc(sy.dma_start(out=bg2b[:, :], in_=I["bg2"]), 16)
        for x in "qkvw":
            ld.inc(sy.dma_start(out=cvb[x][:, :], in_=I["c" + x]), 16)
    blk.sync(p_sl)

    def p_memset(ve):
        for buf in xa + xb:
            ins = ve.memset(buf[:, 0:3], 0.0)
        s_ms.inc(ins)
    blk.vector(p_memset)

    it = [0]
    spn = [0]
    se_batch = []

    def flush_se():
        if se_batch:
            def sec(se, items=tuple(se_batch)):
                for f in items:
                    f(se)
            blk.scalar(sec)
            se_batch.clear()

    def emit_qkvw(X, wt, xrow, cb, tb, par):
        pp = it[0] % 4
        n = it[0]

        def mm(te):
            for s in s_hf:
                s.wait(te)
            s_wl.wait(te)
            s_ms.wait(te)
            if n >= 4:
                s_cv.wait_at(te, n - 3)
            for dc in range(16):
                ins = te.matmul(psP[pp][:, :],
                                wt[:, dc * HC + cb * 128:dc * HC + (cb + 1) * 128],
                                hfm[par][:, dc * TBLK:(dc + 1) * TBLK],
                                start=(dc == 0), stop=(dc == 15))
            s_mm.inc(ins)
        blk.tensor(mm)

        def ev(ve):
            s_mm.wait(ve)
            if n >= 4:
                s_si.wait_at(ve, spn[0] - 3)
            x = xrow[cb]
            if tb > 0:
                hop(ve, ve.tensor_copy(x[:, 0:3], x[:, TBLK:TBLK + 3]),
                    "hopV")
            hop(ve, ve.tensor_copy(x[:, 3:3 + TBLK], psP[pp][:, :]), "hopV")
            cc = cvb[X]
            hop(ve, ve.tensor_scalar_mul(accb[pp][:, :], x[:, 3:3 + TBLK],
                                         cc[:, cb * 4 + 3:cb * 4 + 4]), "hopV")
            for i in (2, 1, 0):
                ins = ve.scalar_tensor_tensor(
                    accb[pp][:, :], x[:, i:i + TBLK],
                    cc[:, cb * 4 + i:cb * 4 + i + 1],
                    accb[pp][:, :], ALU.mult, ALU.add)
                if i > 0:
                    hop(ve, ins, "hopV")
            s_cv.inc(ins)
        blk.vector(ev)

        slot = spn[0] % 6
        ns = spn[0]
        ncv = s_cv.n
        nsp = s_sp[slot].n

        def si(se, slot=slot, pp=pp, ncv=ncv, nsp=nsp):
            s_cv.wait_at(se, ncv)
            s_sp[slot].wait_at(se, nsp)
            s_si.inc(se.activation(spb[slot][:, :], accb[pp][:, :], AF.Silu))
        se_batch.append(si)

        def spdma(sy, ns=ns, slot=slot):
            s_si.wait_at(sy, ns + 1)
            s_sp[slot].inc(sy.dma_start(
                out=spl[X][cb * 128:(cb + 1) * 128, tb * TBLK:(tb + 1) * TBLK],
                in_=spb[slot][:, :]), 16)
        blk.sync(spdma)
        it[0] += 1
        spn[0] += 1

    def emit_fgate(which, wa, wsmall, dstname, tb, par):
        def mm1(te):
            for s in s_hf:
                s.wait(te)
            s_wl.wait(te)
            if s_f1e.n:
                s_f1e.wait(te)
            for dc in range(16):
                ins = te.matmul(psF[:, :], wa[:, dc * 128:(dc + 1) * 128],
                                hfm[par][:, dc * TBLK:(dc + 1) * TBLK],
                                start=(dc == 0), stop=(dc == 15))
            s_f1m.inc(ins)
        blk.tensor(mm1)

        buf = f1b if which == "f" else g1b

        def ev1(ve):
            s_f1m.wait(ve)
            s_hm.wait(ve)
            s_f1e.inc(ve.tensor_copy(buf[:, :], psF[:, :]))
        blk.vector(ev1)

        for h in range(H):
            slot = spn[0] % 6
            ns = spn[0]
            nsp = s_sp[slot].n

            def mm2(te, h=h, ns=ns, slot=slot, nsp=nsp):
                s_f1e.wait(te)
                s_sp[slot].wait_at(te, nsp)
                s_si.wait_at(te, ns)
                ins = te.matmul(psX2[:, :], wsmall[:, h * 128:(h + 1) * 128],
                                buf[:, :], start=True, stop=True)
                s_hm.inc(ins)
            blk.tensor(mm2)
            nhm = s_hm.n

            def act2(se, h=h, slot=slot, which=which, nhm=nhm):
                s_hm.wait_at(se, nhm)
                if which == "f":
                    hop = S("sec")
                    hop.inc(se.activation(ebuf[:, :], psX2[:, :], AF.Exp,
                                          scale=-1.0))
                    hop.wait(se)
                    s_si.inc(se.activation(spb[slot][:, :], ebuf[:, :],
                                           AF.Ln, bias=1.0))
                else:
                    s_si.inc(se.activation(spb[slot][:, :], psX2[:, :],
                                           AF.Sigmoid, bias=bg2b[:, h:h + 1]))
            se_batch.append(act2)

            def spdma(sy, h=h, slot=slot, ns=ns, tb=tb, dstname=dstname):
                s_si.wait_at(sy, ns + 1)
                s_sp[slot].inc(sy.dma_start(
                    out=spl[dstname][h * 128:(h + 1) * 128,
                                     tb * TBLK:(tb + 1) * TBLK],
                    in_=spb[slot][:, :]), 16)
            blk.sync(spdma)
            spn[0] += 1

    gbt = [0]

    def emit_beta(tb, par):
        for tt in range(4):
            def mmb(te, tt=tt, nbt=gbt[0]):
                for s in s_hf:
                    s.wait(te)
                s_bt.wait_at(te, nbt)
                for dc in range(16):
                    ins = te.matmul(
                        psB[:, :],
                        hfm[par][:, dc * TBLK + tt * 128:dc * TBLK + (tt + 1) * 128],
                        wbsb[:, dc * H:(dc + 1) * H],
                        start=(dc == 0), stop=(dc == 15))
                s_mm.inc(ins)
            blk.tensor(mmb)
            nmmb = s_mm.n

            def actb(se, tt=tt, tb=tb, nmmb=nmmb):
                s_mm.wait_at(se, nmmb)
                s_bt.inc(se.activation(
                    P["beta_tm"][:, (tb * 4 + tt) * H:(tb * 4 + tt + 1) * H],
                    psB[:, :], AF.Sigmoid))
            se_batch.append(actb)
            gbt[0] += 1

    for pass_no in (1, 2):
        if pass_no == 2:
            def reload(gp):
                s_te.wait(gp)
                s_wl.inc(gp.dma_start(out=wbuf[0][:, :], in_=I["wv"]), 16)
                s_wl.inc(gp.dma_start(out=wbuf[1][:, :], in_=I["ww"]), 16)
            blk.gpsimd(reload)

            def remset(ve):
                for s in s_sp:
                    s.wait(ve)
                for buf in xa + xb:
                    ins = ve.memset(buf[:, 0:3], 0.0)
                s_ms.inc(ins)
            blk.vector(remset)

        tensors = ([("q", wbuf[0], xa), ("k", wbuf[1], xb)] if pass_no == 1
                   else [("v", wbuf[0], xa), ("w", wbuf[1], xb)])

        for tb in range(NTB):
            par = (tb + (0 if pass_no == 1 else NTB)) % 3

            def tr(sy, tb=tb, par=par):
                s_hw[0].wait_at(sy, 16 * min(16, 2 * tb + 2))
                s_hw[1].wait_at(sy, 16 * min(16, 2 * tb + 2))
                gtb = tb + (0 if pass_no == 1 else NTB)
                need = s_te.n - 2 if gtb >= 3 else 0
                s_te.wait_at(sy, need)
                for j in range(16):
                    s_hf[par].inc(sy.dma_start(
                        out=hfm[par][:, j * TBLK:(j + 1) * TBLK],
                        in_=h16[tb * TBLK:(tb + 1) * TBLK, j * 128:(j + 1) * 128],
                        transpose=True), 16)
            blk.sync(tr)

            for X, wt, xrow in tensors:
                for cb in range(H):
                    emit_qkvw(X, wt, xrow, cb, tb, par)
            if pass_no == 1:
                flush_se()

            if pass_no == 2:
                emit_fgate("f", w128, wsm, "g", tb, par)
                emit_fgate("g1", w128b, wsm2, "gt", tb, par)
                emit_beta(tb, par)
                flush_se()

            def tedone(te):
                s_mm.wait(te)
                s_hm.wait(te)
                s_te.inc(te.nop())
            blk.tensor(tedone)

    def join(sy):
        for s in s_sp:
            s.wait(sy)
    blk.sync(join)

    def join2(se):
        s_bt.wait(se)
    blk.scalar(join2)
    _barrier(blk, S, "proj")
    ps.close()


def _scan(nc, blk, S, I, P, spl, xchg, bar_a, bar_b, dbg_o=None, nch=NCH,
          do_barrier=True):
    hop = _mkhop(S)
    from contextlib import ExitStack
    ps = ExitStack()
    sb = lambda name, shape, dt: ps.enter_context(
        nc.sbuf_tensor(name, shape, dt))
    pst = lambda name, shape, dt: ps.enter_context(
        nc.psum_tensor(name, shape, dt))

    qf = [sb(f"qf{i}", [128, H * TBLK], BF16) for i in range(2)]
    kf = [sb(f"kf{i}", [128, H * TBLK], BF16) for i in range(2)]
    gtf = [sb(f"gtf{i}", [128, H * TBLK], BF16) for i in range(2)]
    gfm = [sb(f"gfm{i}", [128, H * 128], BF16) for i in range(2)]
    w2t_ = [sb(f"w2t{i}", [128, HC], BF16) for i in range(2)]
    sst_ = [sb(f"sst{i}", [128, H], F32) for i in range(2)]
    gs0_ = [sb(f"gs0{i}", [128, H], F32) for i in range(2)]
    gs1_ = [sb(f"gs1{i}", [128, H], F32) for i in range(2)]
    rsqt = [sb(f"rsqt{i}", [128, H], F32) for i in range(2)]
    brst = [sb(f"brst{i}", [128, H], F32) for i in range(2)]
    lcE = [sb(f"lcE{i}", [128, 2 * H], F32) for i in range(2)]
    ktm = [sb(f"ktm{i}", [128, HC], BF16) for i in range(2)]
    vtm = [sb(f"vtm{i}", [128, HC], BF16) for i in range(2)]
    wtm = [sb(f"wtm{i}", [128, HC], BF16) for i in range(2)]
    gtm = [sb(f"gtm{i}", [128, HC], BF16) for i in range(2)]
    Sk = [sb(f"Sk{i}", [128, HC], BF16) for i in range(2)]
    Sv = [sb(f"Sv{i}", [128, HC], BF16) for i in range(2)]
    ostg = [sb(f"ostg{i}", [128, H * TBLK], BF16) for i in range(2)]

    d2 = lambda name, shape, dt=BF16: [sb(name + str(i), shape, dt)
                                       for i in range(2)]
    Eb = d2("Eb", [128, HC]); Ab = d2("Ab", [128, HC])
    Ub = d2("Ub", [128, HC]); w2b = d2("w2b", [128, HC])
    kqs = d2("kqs", [128, H * C]); Xs = d2("Xs", [128, HC])
    bcs = d2("bcs", [128, HC]); sbf = d2("sbf", [128, HC])
    s2b = d2("s2b", [128, HC]); eeb = d2("eeb", [128, HC])
    Pb = d2("Pb", [128, HC]); KQm = d2("KQm", [128, H * C])
    w2s = d2("w2s", [128, H * C]); W2Tm = d2("W2Tm", [128, H * C])
    UPT = d2("UPT", [128, 2 * H * C]); O2b = d2("O2b", [128, H * C])
    Osb = d2("Osb", [128, H * C]); rbs = d2("rbs", [128, H * C])
    Oab = d2("Oab", [128, H * C])
    ssb = d2("ssb", [128, H], F32); t8a = d2("t8a", [128, H], F32)
    rsqb = d2("rsqb", [128, H], F32); brsq = d2("brsq", [128, H], F32)
    mxb = d2("mxb", [128, H], F32); smb = d2("smb", [128, H], F32)
    rrb = d2("rrb", [128, H], F32); lcol = d2("lcol", [128, H], F32)
    lrow = d2("lrow", [1, HC], BF16)
    rfr = d2("rfr", [1, H * C], F32); rfr2 = d2("rfr2", [1, H * C], BF16)

    PS1 = pst("PS1", [128, HC], F32)
    PS3 = pst("PS3", [128, H * C], F32)
    PS4 = pst("PS4", [128, H * C], F32)
    PS5 = pst("PS5", [128, 2 * H * C], BF16)
    PS6 = pst("PS6", [128, 512], F32)

    s_fm = [S("s_fm0"), S("s_fm1")]
    s_tm = [S("s_tm0"), S("s_tm1")]
    s_ost = [S("s_ost0"), S("s_ost1")]
    s_bar = S("s_bar")

    A_ = {}
    for nm in ("a1", "a2", "a3", "a4", "a5", "a6", "a7", "a8", "a9", "a10",
               "a11", "a12", "a13", "a14", "a15", "a16", "a17", "a18", "a19",
               "a20", "a21", "a22", "a23", "a25", "a26", "a28", "a4",
               "b1", "b2", "b3", "b4"):
        A_[nm] = S("sc_" + nm)

    tub, tumb, idb, onec, oner = (P["tub"], P["tumb"], P["idb"], P["onec"],
                                  P["oner"])
    beta_tm = P["beta_tm"]
    v3 = lambda t: t[:, :].rearrange("p (hh x) -> p hh x", hh=H)

    def init(ve):
        S("c_ld").wait(ve)
        A_["a28"].inc(ve.memset(Sk[1][:, :], 0.0))
        A_["a25"].inc(ve.memset(Sv[1][:, :], 0.0))
    blk.vector(init)
    ST0 = 1

    def init_te(te):
        S("c_ld").wait(te)
    blk.tensor(init_te)

    fm_cnt = [0, 0]
    tm_cnt = [0, 0]
    fm_ld = fm_cnt
    tm_ld = tm_cnt

    def chunk_done_at(eng, n):
        A_["a22"].wait_at(eng, n)
        A_["a26"].wait_at(eng, n)
        A_["a28"].wait_at(eng, ST0 + n)

    def load_fm(blkid):
        def f(sy, blkid=blkid):
            if blkid >= 2:
                chunk_done_at(sy, 8 * (blkid - 1))
            bp = blkid % 2
            for name, buf in (("q", qf), ("k", kf), ("gt", gtf)):
                srcv = spl[name].rearrange("(hh p) t -> p hh t", p=128)
                s_fm[bp].inc(sy.dma_start(
                    out=buf[bp][:, :].rearrange("p (hh t) -> p hh t", hh=H),
                    in_=srcv[:, :, blkid * TBLK:(blkid + 1) * TBLK]), 16)
            fm_cnt[bp] = s_fm[bp].n
        blk.sync(f)

    def load_tm(tt):
        def f(sy, tt=tt):
            if tt >= 2:
                chunk_done_at(sy, 2 * (tt - 1))
            tp = tt % 2
            for name, buf in (("k", ktm), ("v", vtm), ("w", wtm), ("g", gtm)):
                s_tm[tp].inc(sy.dma_start(
                    out=buf[tp][:, :],
                    in_=spl[name][:, tt * 128:(tt + 1) * 128],
                    transpose=True), 16)
            srcg = spl["g"].rearrange("(hh p) t -> p hh t", p=128)
            s_tm[tp].inc(sy.dma_start(
                out=gfm[tp][:, :].rearrange("p (hh t) -> p hh t", hh=H),
                in_=srcg[:, :, tt * 128:(tt + 1) * 128]), 16)
            tm_cnt[tp] = s_tm[tp].n
        blk.sync(f)

    load_fm(0)
    load_fm(1)
    load_tm(0)
    load_tm(1)

    te_as, te_bs, te_eps = [], [], []
    se_as, se_bs, se_eps = [], [], []
    ve_as, ve_bs, ve_eps = [], [], []

    def w_at(eng, nm, n):
        A_[nm].wait_at(eng, n)

    def te_s1(te, c, R, tp, bp, tib, fmc, tmc):
        g3 = gtm[tp]
        ro = C if R.start == 0 else 0
        s_fm[bp].wait_at(te, fmc)
        s_tm[tp].wait_at(te, tmc)
        w_at(te, "a25", ST0 + c)
        w_at(te, "a7", c)
        w_at(te, "a2", c)
        te.matmul(PS1[R, 0:512], tub[R, :], g3[R, 0:512],
                  start=True, stop=True)
        te.matmul(PS1[R, 512:1024], tub[R, :], g3[R, 512:1024],
                  start=True, stop=True)
        te.matmul(PS1[ro:ro + 1, 0:512], onec[R, :], g3[R, 0:512],
                  start=True, stop=True)
        te.matmul(PS1[ro:ro + 1, 512:1024], onec[R, :], g3[R, 512:1024],
                  start=True, stop=True)
        for h in range(H):
            ins = te.matmul(PS3[R, h * C:(h + 1) * C],
                            kf[bp][:, h * TBLK + tib:h * TBLK + tib + C],
                            qf[bp][:, h * TBLK + tib:h * TBLK + tib + C],
                            start=True, stop=True)
        A_["a1"].inc(ins)

    for cix in range(nch):
        tt = cix // 2
        beta = cix % 2
        blkid = cix // 8
        bp = blkid % 2
        tp = tt % 2
        cp = cix % 2
        tib = (cix * C) % TBLK
        R = slice(beta * C, beta * C + C)
        cur, nxt = (cix + 1) % 2, cix % 2
        c = cix
        fmc = fm_cnt[bp]
        tmc = tm_cnt[tp]
        if cix + 1 < nch:
            fmc2 = fm_cnt[((cix + 1) // 8) % 2]
            tmc2 = tm_cnt[((cix + 1) // 2) % 2]
        else:
            fmc2 = tmc2 = 0
        ostn = s_ost[(cix // 8) % 2].n if cix // 8 >= 2 else 0

        if cix == 0:
            def te_pro(te, fmc=fmc, tmc=tmc):
                te_s1(te, 0, slice(0, C), 0, 0, 0, fmc, tmc)
            blk.tensor(te_pro)

        def te_a(te, c=c, R=R, tp=tp, bp=bp, tib=tib, cur=cur, cp=cp):
            w_at(te, "a5", c + 1)
            w_at(te, "a12", c)
            for h in range(H):
                te.matmul(PS1[R, h * 128:(h + 1) * 128],
                          KQm[cp][R, h * C:(h + 1) * C],
                          Ub[cp][R, h * 128:(h + 1) * 128],
                          start=True, stop=False)
                te.matmul(PS1[R, h * 128:(h + 1) * 128],
                          qf[bp][:, h * TBLK + tib:h * TBLK + tib + C],
                          Sk[cur][:, h * 128:(h + 1) * 128],
                          start=False, stop=True)
            for h in range(H):
                ins = te.transpose(PS5[:, h * C:(h + 1) * C],
                                   Ub[cp][R, h * 128:(h + 1) * 128], idb[R, :])
            A_["a6"].inc(ins)

        def te_b(te, c=c, R=R, tp=tp, bp=bp, tib=tib, cur=cur, cp=cp,
                 fmc2=fmc2, tmc2=tmc2):
            w_at(te, "a2", c + 1)
            w_at(te, "a17", c)
            te.matmul(PS3[:, 0:512], oner[0:1, :], lrow[cp][0:1, 0:512],
                      start=True, stop=True)
            ins = te.matmul(PS4[:, 0:512], oner[0:1, :],
                            lrow[cp][0:1, 512:1024], start=True, stop=True)
            A_["a4"].inc(ins)
            w_at(te, "a7", c + 1)
            w_at(te, "a21", c)
            for h in range(H):
                te.matmul(PS1[:, h * 128:(h + 1) * 128],
                          ktm[tp][R, h * 128:(h + 1) * 128],
                          Ub[cp][R, h * 128:(h + 1) * 128],
                          start=True, stop=True)
            for h in range(4):
                ins = te.matmul(PS6[:, h * 128:(h + 1) * 128],
                                Ub[cp][R, h * 128:(h + 1) * 128],
                                vtm[tp][R, h * 128:(h + 1) * 128],
                                start=True, stop=True)
            A_["a23"].inc(ins)
            w_at(te, "a10", c + 1)
            for h in range(H):
                ins = te.transpose(PS5[:, (H + h) * C:(H + h + 1) * C],
                                   Pb[cp][R, h * 128:(h + 1) * 128], idb[R, :])
            A_["a11"].inc(ins)
            w_at(te, "a12", c + 1)
            w_at(te, "a25", ST0 + c + 1)
            w_at(te, "a21", c)
            for h in range(H):
                ins = te.matmul(PS6[R, h * 64:(h + 1) * 64],
                                UPT[cp][:, h * C:(h + 1) * C],
                                UPT[cp][:, (H + h) * C:(H + h + 1) * C],
                                start=True, stop=True)
            A_["a13"].inc(ins)
            w_at(te, "a15", c + 1)
            w_at(te, "a28", ST0 + c)
            w_at(te, "a17", c)
            for h in range(H):
                te.matmul(PS4[:, h * C:(h + 1) * C],
                          Sv[cur][:, h * 128:(h + 1) * 128],
                          UPT[cp][:, (H + h) * C:(H + h + 1) * C],
                          start=True, stop=False)
                ins = te.matmul(PS4[:, h * C:(h + 1) * C],
                                vtm[tp][R, h * 128:(h + 1) * 128],
                                W2Tm[cp][R, h * C:(h + 1) * C],
                                start=False, stop=True)
            A_["a16"].inc(ins)
            if c + 1 < nch:
                c2 = c + 1
                te_s1(te, c2, slice((c2 % 2) * C, (c2 % 2) * C + C),
                      (c2 // 2) % 2, (c2 // 8) % 2, (c2 * C) % TBLK,
                      fmc2, tmc2)
            w_at(te, "a25", ST0 + c + 1)
            w_at(te, "a14", c + 1)
            for h in range(4, 8):
                ins = te.matmul(PS6[:, (h - 4) * 128:(h - 3) * 128],
                                Ub[cp][R, h * 128:(h + 1) * 128],
                                vtm[tp][R, h * 128:(h + 1) * 128],
                                start=True, stop=True)
            A_["a26"].inc(ins)

        def te_ep(te, c=c, cp=cp):
            w_at(te, "a17", c + 1)
            w_at(te, "a28", ST0 + c + 1)
            for h in range(H):
                ins = te.matmul(PS6[0:1, h * 64:(h + 1) * 64], onec[:, :],
                                O2b[cp][:, h * C:(h + 1) * C],
                                start=True, stop=True)
            A_["a18"].inc(ins)
            w_at(te, "a19", c + 1)
            ins = te.matmul(PS6[:, :], oner[0:1, :], rfr2[cp][0:1, :],
                            start=True, stop=True)
            A_["a20"].inc(ins)
        te_as.append(te_a)
        te_bs.append(te_b)
        te_eps.append(te_ep)

        def se_a(se, c=c, R=R, tp=tp, cp=cp, tt=tt, beta=beta):
            S("c_cc").wait(se)
            if beta == 0:
                w_at(se, "a1", c + 1)
                w_at(se, "b2", tt - 1)
                ins = se.activation(w2t_[tp][:, :], wtm[tp][:, :], AF.Square)
                A_["b1"].inc(ins)
            w_at(se, "a1", c + 1)
            w_at(se, "a5", c - 1)
            w_at(se, "a10", c - 1)
            w_at(se, "a6", c - 1)
            w_at(se, "a3", c - 1)
            ro = C if beta == 0 else 0
            se.activation(Eb[cp][R, :], PS1[R, :], AF.Exp)
            se.activation(Ab[cp][R, :], PS1[R, :], AF.Exp, scale=-1.0,
                          bias=P["cc"][R, 0:1])
            se.activation(lrow[cp][0:1, :], PS1[ro:ro + 1, :], AF.Exp,
                          scale=-1.0)
            ins = se.activation(kqs[cp][R, :], PS3[R, :], AF.Copy)
            A_["a2"].inc(ins)
            if beta == 0:
                w_at(se, "b2", tt + 1)
                w_at(se, "b4", tt - 1)
                w_at(se, "a28", ST0 + 2 * tt - 2)
                hop(se, se.activation(sst_[tp][:, :], sst_[tp][:, :], AF.Ln,
                                      bias=P["cc"][:, 1:2]), "hopS")
                se.activation(rsqt[tp][:, :], sst_[tp][:, :], AF.Exp,
                              scale=P["cc"][:, 3:4])
                se.activation(lcE[tp][:, 0:H], gs0_[tp][:, :], AF.Exp,
                              scale=-1.0)
                ins = se.activation(lcE[tp][:, H:2 * H], gs1_[tp][:, :],
                                    AF.Exp, scale=-1.0)
                A_["b3"].inc(ins)

        def se_b(se, c=c, R=R, tp=tp, cp=cp, tt=tt, beta=beta):
            w_at(se, "a6", c + 1)
            w_at(se, "a4", c + 1)
            w_at(se, "a8", c - 1)
            w_at(se, "a25", ST0 + c)
            se.activation(Xs[cp][R, :], PS1[R, :], AF.Copy)
            se.activation(bcs[cp][:, 0:512], PS3[:, 0:512], AF.Copy)
            ins = se.activation(bcs[cp][:, 512:1024], PS4[:, 0:512], AF.Copy)
            A_["a7"].inc(ins)
            w_at(se, "a8", c + 1)
            w_at(se, "a10", c - 1)
            ins = se.activation(eeb[cp][R, :], s2b[cp][R, :], AF.Exp)
            A_["a9"].inc(ins)
            w_at(se, "a13", c + 1)
            w_at(se, "a15", c - 1)
            ins = se.activation(w2s[cp][R, :], PS6[R, 0:512], AF.Copy)
            A_["a14"].inc(ins)
            w_at(se, "a16", c + 1)
            w_at(se, "a18", c - 1)
            w_at(se, "a22", c - 1)
            se.activation(O2b[cp][:, :], PS4[:, :], AF.Square)
            ins = se.activation(Osb[cp][:, :], PS4[:, :], AF.Copy)
            A_["a17"].inc(ins)

        def se_ep(se, c=c, cp=cp):
            w_at(se, "a18", c + 1)
            w_at(se, "a20", c - 1)
            hop(se, se.activation(rfr[cp][0:1, :], PS6[0:1, :], AF.Ln,
                                  scale=P["cc"][0:1, 4:5],
                                  bias=P["cc"][0:1, 2:3]), "hopS")
            ins = se.activation(rfr2[cp][0:1, :], rfr[cp][0:1, :], AF.Exp,
                                scale=P["cc"][0:1, 3:4])
            A_["a19"].inc(ins)
            w_at(se, "a20", c + 1)
            w_at(se, "a22", c - 1)
            ins = se.activation(rbs[cp][:, :], PS6[:, :], AF.Copy)
            A_["a21"].inc(ins)
        se_as.append(se_a)
        se_bs.append(se_b)
        se_eps.append(se_ep)

        def ve_a(ve, c=c, R=R, tp=tp, cp=cp, tt=tt, beta=beta):
            if beta == 0:
                w_at(ve, "b1", tt + 1)
                w_at(ve, "b3", tt - 1)
                ve.reduce_sum(sst_[tp][:, :], v3(w2t_[tp])[:, :, :], axis=AX.X)
                gv = gfm[tp][:, :].rearrange("p (hh t) -> p hh t", hh=H)
                ve.reduce_sum(gs0_[tp][:, :], gv[:, :, 0:C], axis=AX.X)
                ins = ve.reduce_sum(gs1_[tp][:, :], gv[:, :, C:2 * C],
                                    axis=AX.X)
                A_["b2"].inc(ins)
            w_at(ve, "a2", c + 1)
            w_at(ve, "a6", c - 1)
            ins = ve.tensor_tensor(
                v3(KQm[cp])[R, :, :],
                kqs[cp][R, :].rearrange("p (hh x) -> p hh x", hh=H),
                tumb[R, :].rearrange("p (o x) -> p o x", o=1).broadcast_to(
                    [C, H, C]), ALU.mult)
            A_["a3"].inc(ins)
            if beta == 0:
                w_at(ve, "b3", tt + 1)
                w_at(ve, "a5", 2 * tt - 2)
                ins = ve.tensor_mul(brst[tp][:, :],
                                    beta_tm[:, tt * H:(tt + 1) * H],
                                    rsqt[tp][:, :])
                A_["b4"].inc(ins)
            w_at(ve, "b4", tt + 1)
            w_at(ve, "a26", c - 1)
            hop(ve, ve.tensor_mul(Ub[cp][R, :], Eb[cp][R, :], wtm[tp][R, :]),
                "hopV")
            ins = ve.tensor_tensor(
                v3(Ub[cp])[R, :, :], v3(Ub[cp])[R, :, :],
                brst[tp][R, :].rearrange("p (hh o) -> p hh o",
                                         o=1).broadcast_to([C, H, 128]),
                ALU.mult)
            A_["a5"].inc(ins)

        def ve_b(ve, c=c, R=R, tp=tp, cur=cur, nxt=nxt, cp=cp, tt=tt,
                 beta=beta):
            w_at(ve, "a7", c + 1)
            w_at(ve, "a9", c - 1)
            hop(ve, ve.tensor_mul(sbf[cp][R, :], Xs[cp][R, :], Ab[cp][R, :]),
                "hopV")
            hop(ve, ve.reduce_max(mxb[cp][R, :], v3(sbf[cp])[R, :, :],
                                  axis=AX.X), "hopV")
            ins = ve.tensor_tensor(
                v3(s2b[cp])[R, :, :], v3(sbf[cp])[R, :, :],
                mxb[cp][R, :].rearrange("p (hh o) -> p hh o",
                                        o=1).broadcast_to([C, H, 128]),
                ALU.subtract)
            A_["a8"].inc(ins)
            w_at(ve, "a9", c + 1)
            w_at(ve, "a11", c - 1)
            ve.reduce_sum(smb[cp][R, :], v3(eeb[cp])[R, :, :], axis=AX.X)
            hop(ve, ve.tensor_mul(Pb[cp][R, :], eeb[cp][R, :], Ab[cp][R, :]),
                "hopV")
            hop(ve, ve.reciprocal(rrb[cp][R, :], smb[cp][R, :]), "hopV")
            hop(ve, ve.tensor_scalar_mul(rrb[cp][R, :], rrb[cp][R, :],
                                         P["cc"][R, 7:8]), "hopV")
            ins = ve.tensor_tensor(
                v3(Pb[cp])[R, :, :], v3(Pb[cp])[R, :, :],
                rrb[cp][R, :].rearrange("p (hh o) -> p hh o",
                                        o=1).broadcast_to([C, H, 128]),
                ALU.mult)
            A_["a10"].inc(ins)
            w_at(ve, "a11", c + 1)
            w_at(ve, "a16", c - 1)
            ins = ve.tensor_copy(UPT[cp][:, :], PS5[:, :])
            A_["a12"].inc(ins)
            w_at(ve, "a23", c + 1)
            w_at(ve, "a6", c)
            hop(ve, ve.tensor_tensor(Sk[nxt][:, :], Sk[cur][:, :], PS1[:, :],
                                     ALU.add), "hopV")
            hop(ve, ve.tensor_mul(Sk[nxt][:, :], Sk[nxt][:, :],
                                  bcs[cp][:, :]), "hopV")
            ins = ve.tensor_tensor(Sv[nxt][:, 0:512], Sv[cur][:, 0:512],
                                   PS6[:, :], ALU.add)
            A_["a25"].inc(ins)
            w_at(ve, "a14", c + 1)
            w_at(ve, "a16", c - 1)
            ins = ve.tensor_tensor(
                v3(W2Tm[cp])[R, :, :],
                w2s[cp][R, :].rearrange("p (hh x) -> p hh x", hh=H),
                tumb[R, :].rearrange("p (o x) -> p o x", o=1).broadcast_to(
                    [C, H, C]), ALU.mult)
            A_["a15"].inc(ins)
            w_at(ve, "a26", c + 1)
            w_at(ve, "a16", c)
            w_at(ve, "b3", tt + 1)
            hop(ve, ve.tensor_tensor(Sv[nxt][:, 512:1024],
                                     Sv[cur][:, 512:1024], PS6[:, :],
                                     ALU.add), "hopV")
            ins = ve.tensor_tensor(
                v3(Sv[nxt])[:, :, :], v3(Sv[nxt])[:, :, :],
                lcE[tp][:, beta * H:(beta + 1) * H].rearrange(
                    "p (hh o) -> p hh o", o=1).broadcast_to([128, H, 128]),
                ALU.mult)
            A_["a28"].inc(ins)

        def ve_ep(ve, c=c, cp=cp, cix=cix, tib=tib, bp=bp, ostn=ostn):
            w_at(ve, "a21", c + 1)
            w_at(ve, "a17", c + 1)
            if cix // 8 >= 2:
                s_ost[(cix // 8) % 2].wait_at(ve, ostn)
            hop(ve, ve.tensor_mul(Oab[cp][:, :], Osb[cp][:, :],
                                  rbs[cp][:, :]), "hopV")
            os = (cix // 8) % 2
            ins = ve.tensor_tensor(
                v3(ostg[os])[:, :, tib:tib + C],
                Oab[cp][:, :].rearrange("p (hh x) -> p hh x", hh=H),
                v3(gtf[bp])[:, :, tib:tib + C], ALU.mult)
            A_["a22"].inc(ins)
        ve_as.append(ve_a)
        ve_bs.append(ve_b)
        ve_eps.append(ve_ep)

        if cix % 8 == 7:
            def oflush(sy, cix=cix):
                A_["a22"].wait_at(sy, cix + 1)
                os = (cix // 8) % 2
                blkid2 = cix // 8
                xv = [xchg[i, :, :].rearrange("p (hh t) -> p hh t", hh=H)
                      for i in range(2)]
                with sy.register(f"par_s{cix}") as reg:
                    sy.reg_load(reg, I["parity"][0:1, 0:1])
                    with sy.If_eq(reg, 0):
                        s_ost[os].inc(sy.dma_start(
                            out=xv[0][:, :, blkid2 * TBLK:(blkid2 + 1) * TBLK],
                            in_=v3(ostg[os])[:, :, :]), 16)
                    with sy.Else():
                        sy.dma_start(
                            out=xv[1][:, :, blkid2 * TBLK:(blkid2 + 1) * TBLK],
                            in_=v3(ostg[os])[:, :, :]).then_inc(
                                s_ost[os].h, 16)
                if dbg_o is not None:
                    dv = dbg_o.rearrange("p (hh t) -> p hh t", hh=H)
                    s_ost[os].inc(sy.dma_start(
                        out=dv[:, :, blkid2 * TBLK:(blkid2 + 1) * TBLK],
                        in_=v3(ostg[os])[:, :, :]), 16)
            blk.sync(oflush)

        if cix % 2 == 1 and tt + 2 < 32:
            load_tm(tt + 2)
        if cix % 8 == 7 and blkid + 2 < NTB:
            load_fm(blkid + 2)

    GRP = 8
    for g0 in range(0, nch, GRP):
        cs = list(range(g0, min(g0 + GRP, nch)))

        def te_sec(te, cs=cs):
            for c2 in cs:
                te_as[c2](te)
                if c2 >= 1:
                    te_eps[c2 - 1](te)
                te_bs[c2](te)
        blk.tensor(te_sec)

        def se_sec(se, cs=cs):
            for c2 in cs:
                se_as[c2](se)
                if c2 >= 1:
                    se_eps[c2 - 1](se)
                se_bs[c2](se)
        blk.scalar(se_sec)

        def ve_sec(ve, cs=cs):
            for c2 in cs:
                ve_as[c2](ve)
                if c2 >= 1:
                    ve_eps[c2 - 1](ve)
                ve_bs[c2](ve)
        blk.vector(ve_sec)

    def te_tail(te):
        te_eps[nch - 1](te)
    blk.tensor(te_tail)

    def se_tail(se):
        se_eps[nch - 1](se)
    blk.scalar(se_tail)

    def ve_tail(ve):
        ve_eps[nch - 1](ve)
    blk.vector(ve_tail)

    def sjoin(sy):
        s_ost[0].wait(sy)
        s_ost[1].wait(sy)
    blk.sync(sjoin)

    if do_barrier:
        def barrier(gp):
            s_ost[0].wait(gp)
            s_ost[1].wait(gp)
            s_bar.inc(gp.collective_compute(
                "AllReduce", ALU.add, replica_groups=[list(range(8))],
                ins=[bar_a], outs=[bar_b]))
            s_bar.wait(gp)
        blk.gpsimd(barrier)
    _barrier(blk, S, "scan")
    ps.close()


def _wo(nc, blk, S, I, P, xchg, x1_32, x1_16, prefetched=False):
    hop = _mkhop(S)
    from contextlib import ExitStack
    ps = ExitStack()
    sb = lambda name, shape, dt: ps.enter_context(
        nc.sbuf_tensor(name, shape, dt))
    pst = lambda name, shape, dt: ps.enter_context(
        nc.psum_tensor(name, shape, dt))

    wo_sb = sb("wo_sb", [128, 16 * D], BF16)
    om = [sb(f"om{i}", [128, HC], BF16) for i in range(2)]
    opb = [sb(f"opb{i}", [128, HC], BF16) for i in range(2)]
    hres = [sb(f"hres{i}", [128, D], F32) for i in range(2)]
    z1 = sb("z1", [128, D], F32)
    x1f = sb("x1f", [128, D], F32)
    x1b = sb("x1b", [128, D], BF16)
    msum = sb("msum", [128, 1], F32)
    nmean = sb("nmean", [128, 1], F32)
    ssq = sb("ssq", [128, 1], F32)
    em2 = sb("em2", [128, 1], F32)
    vtmp = sb("vtmp", [128, 1], F32)
    rstd = sb("rstd", [128, 1], F32)
    sqs = sb("sqs", [128, D], BF16)
    PW = [pst(f"PW{i}", [128, D], F32) for i in range(2)]

    s_wg = S("w_wg")
    s_wsc = S("w_wsc")
    s_ot = [S("w_ot0"), S("w_ot1")]
    s_hr = [S("w_hr0"), S("w_hr1")]
    s_wmm = S("w_mm")
    s_wev = S("w_ev")
    s_lnA = S("w_lnA")
    s_ln2 = S("w_ln2")
    s_lnB = S("w_lnB")
    s_ln4 = S("w_ln4")
    s_x1d = S("w_x1d")

    def wol(gp):
        s_wg.inc(gp.dma_start(out=wo_sb[:, :], in_=I["wo"]), 16)
    blk.gpsimd(wol)

    def wosc(ve):
        S("c_ld").wait(ve)
        s_wg.wait(ve)
        s_wsc.inc(ve.tensor_scalar_mul(wo_sb[:, :], wo_sb[:, :],
                                       P["normw"][:, 0:1]))
    blk.vector(wosc)

    xv = [xchg[i, :, :].rearrange("p (hh t) -> p hh t", hh=H)
          for i in range(2)]
    wo_seb = []

    for i in range(16):
        pp = i % 2

        def otl(sy, i=i, pp=pp):
            if i >= 2:
                s_wmm.wait_at(sy, i - 1)
                s_wev.wait_at(sy, i - 1)
            with sy.register(f"par_w{i}") as reg:
                sy.reg_load(reg, I["parity"][0:1, 0:1])
                with sy.If_eq(reg, 0):
                    s_ot[pp].inc(sy.dma_start(
                        out=om[pp][:, :].rearrange("p (hh d) -> p hh d", hh=H),
                        in_=xv[0][:, :, i * 128:(i + 1) * 128]), 16)
                    s_ot[pp].inc(sy.dma_start(
                        out=opb[pp][:, :].rearrange("p (hh d) -> p hh d", hh=H),
                        in_=xv[1][:, :, i * 128:(i + 1) * 128]), 16)
                with sy.Else():
                    sy.dma_start(
                        out=om[pp][:, :].rearrange("p (hh d) -> p hh d", hh=H),
                        in_=xv[1][:, :, MYT + i * 128:MYT + (i + 1) * 128]
                    ).then_inc(s_ot[pp].h, 16)
                    sy.dma_start(
                        out=opb[pp][:, :].rearrange("p (hh d) -> p hh d", hh=H),
                        in_=xv[0][:, :, MYT + i * 128:MYT + (i + 1) * 128]
                    ).then_inc(s_ot[pp].h, 16)
            s_hr[pp].inc(sy.dma_start(
                out=hres[pp][:, :],
                in_=I["h_res"][i * 128:(i + 1) * 128, :]), 16)
        blk.sync(otl)

        def wmm(te, i=i, pp=pp):
            s_ot[pp].wait(te)
            s_wsc.wait(te)
            if i >= 2:
                s_wev.wait_at(te, i - 1)
            for cg in range(4):
                for hd in range(16):
                    src = (om[pp][:, (hd % H) * 128:(hd % H + 1) * 128]
                           if hd < H else
                           opb[pp][:, (hd - H) * 128:(hd - H + 1) * 128])
                    ins = te.matmul(
                        PW[pp][:, cg * 512:(cg + 1) * 512], src,
                        wo_sb[:, hd * D + cg * 512:hd * D + (cg + 1) * 512],
                        start=(hd == 0), stop=(hd == 15))
            s_wmm.inc(ins)
        blk.tensor(wmm)

        def wev(ve, i=i, pp=pp):
            s_wmm.wait(ve)
            s_hr[pp].wait(ve)
            if i >= 1:
                s_ln4.wait_at(ve, i)
            hop(ve, ve.tensor_add(z1[:, :], PW[pp][:, :], hres[pp][:, :]),
                "hopV")
            ins = ve.reduce_sum(msum[:, :],
                                z1[:, :].rearrange("p (o d) -> p o d", o=1),
                                axis=AX.X)
            s_wev.inc(ins)
        blk.vector(wev)

        def wln(se, i=i, nv=s_wev.n):
            S("c_cc").wait(se)
            s_wev.wait_at(se, nv)
            se.activation(sqs[:, :], z1[:, :], AF.Square, accum_out=ssq[:, :])
            hop = S("sec")
            hop.inc(se.activation(nmean[:, :], msum[:, :], AF.Copy,
                                  scale=P["cc"][:, 5:6]))
            hop.wait(se)
            s_lnA.inc(se.activation(em2[:, :], nmean[:, :], AF.Square))
        wo_seb.append(wln)

        def wln2(ve, i=i):
            s_lnA.wait_at(ve, i + 1)
            s_ln2.inc(ve.scalar_tensor_tensor(
                vtmp[:, :], ssq[:, :], P["cc"][:, 6:7], em2[:, :],
                ALU.mult, ALU.subtract))
        blk.vector(wln2)

        def wln3(se, i=i):
            s_ln2.wait_at(se, i + 1)
            hop = S("sec")
            hop.inc(se.activation(vtmp[:, :], vtmp[:, :], AF.Ln,
                                  bias=P["cc"][:, 2:3]))
            hop.wait(se)
            s_lnB.inc(se.activation(rstd[:, :], vtmp[:, :], AF.Exp,
                                    scale=P["cc"][:, 3:4]))
        wo_seb.append(wln3)

        def wln4(ve, i=i):
            s_lnB.wait_at(ve, i + 1)
            if i >= 1:
                s_x1d.wait_at(ve, 32 * i)
            hop(ve, ve.tensor_scalar(x1f[:, :], z1[:, :], nmean[:, 0:1],
                                     rstd[:, 0:1], ALU.add, ALU.mult), "hopV")
            s_ln4.inc(ve.tensor_copy(x1b[:, :], x1f[:, :]))
        blk.vector(wln4)

        def wst(sy, i=i):
            s_ln4.wait(sy)
            s_x1d.inc(sy.dma_start(out=x1_32[i * 128:(i + 1) * 128, :],
                                   in_=x1f[:, :]), 16)
            s_x1d.inc(sy.dma_start(out=x1_16[i * 128:(i + 1) * 128, :],
                                   in_=x1b[:, :]), 16)
        blk.sync(wst)

    def wo_sec(se, items=tuple(wo_seb)):
        for f in items:
            f(se)
    blk.scalar(wo_sec)

    def wjoin(sy):
        s_x1d.wait(sy)
    blk.sync(wjoin)
    _barrier(blk, S, "wo")
    ps.close()


def _ffn(nc, blk, S, I, P, x1_32, x1_16, out_t):
    hop = _mkhop(S)
    from contextlib import ExitStack
    ps = ExitStack()
    sb = lambda name, shape, dt: ps.enter_context(
        nc.sbuf_tensor(name, shape, dt))
    pst = lambda name, shape, dt: ps.enter_context(
        nc.psum_tensor(name, shape, dt))

    NB = MYT // FBLK
    xfm = sb("xfm", [128, 16 * FBLK], BF16)
    fuse = sb("fuse", [128, 64 * FBLK], BF16)
    wgb = [sb(f"wgb{i}", [128, 16 * 128], BF16) for i in range(2)]
    wub = [sb(f"wub{i}", [128, 16 * 128], BF16) for i in range(2)]
    wdb = [sb(f"wdb{i}", [128, 16 * 512], BF16) for i in range(2)]
    sgl = [sb(f"sgl{i}", [128, FBLK], BF16) for i in range(2)]
    sgsb = sb("sgsb", [128, 64], F32)
    susb = sb("susb", [128, 64], F32)
    sdro = sb("sdro", [1, D], BF16)
    sdbc = sb("sdbc", [128, D], BF16)
    z2 = [sb(f"z2_{i}", [128, D], F32) for i in range(4)]
    x1r = sb("x1r", [128, D], F32)
    x2b = sb("x2b", [128, D], F32)
    msum = sb("fmsum", [128, 1], F32)
    nmean = sb("fnmean", [128, 1], F32)
    ssq = sb("fssq", [128, 1], F32)
    em2 = sb("fem2", [128, 1], F32)
    vtmp = sb("fvtmp", [128, 1], F32)
    rstd = sb("frstd", [128, 1], F32)
    sqs = sb("fsqs", [128, D], BF16)
    PG = [pst(f"PG{i}", [128, FBLK], F32) for i in range(2)]
    PU = [pst(f"PU{i}", [128, FBLK], F32) for i in range(2)]
    PD = [pst(f"PD{i}", [128, 512], F32) for i in range(4)]
    PSD = PD[0]

    s_fl = S("f_fl")
    s_sd1 = S("f_sd1")
    s_sd2 = S("f_sd2")
    s_fw = S("f_fw")
    s_gu = [S("f_gu0"), S("f_gu1")]
    s_wd = [S("f_wd0"), S("f_wd1")]
    s_amm = S("f_amm")
    s_asi = S("f_asi")
    s_afu = S("f_afu")
    s_dmm = S("f_dmm")
    s_bev = S("f_bev")
    s_x1r = S("f_x1r")
    s_flna = S("f_flna")
    s_l2a = S("f_l2a")
    s_l2v = S("f_l2v")
    s_l2b = S("f_l2b")
    s_ln5 = S("f_ln5")
    s_out = S("f_out")

    def fsl(gp):
        s_fl.inc(gp.dma_start(out=sgsb[:, :], in_=I["sgate"]), 16)
        s_fl.inc(gp.dma_start(out=susb[:, :], in_=I["sup"]), 16)
        s_fl.inc(gp.dma_start(out=sdro[0:1, :], in_=I["sdown"]), 16)
    blk.gpsimd(fsl)

    for cg in range(4):
        def sdm(te, cg=cg):
            S("c_ld").wait(te)
            s_fl.wait(te)
            if cg > 0:
                s_sd2.wait(te)
            s_sd1.inc(te.matmul(PSD[:, :], P["oner"][0:1, :],
                                sdro[0:1, cg * 512:(cg + 1) * 512],
                                start=True, stop=True))
        blk.tensor(sdm)

        def sde(ve, cg=cg):
            s_sd1.wait(ve)
            s_sd2.inc(ve.tensor_copy(sdbc[:, cg * 512:(cg + 1) * 512],
                                     PSD[:, :]))
        blk.vector(sde)

    gic = [0]
    gicg = [0]
    fn_seb = []

    def fn_flush():
        if fn_seb:
            def sec(se, items=tuple(fn_seb)):
                for f in items:
                    f(se)
            blk.scalar(sec)
            fn_seb.clear()

    for b in range(NB):
        def xtr(sy, b=b):
            if b >= 1:
                s_amm.wait(sy)
            for j in range(16):
                s_fw.inc(sy.dma_start(
                    out=xfm[:, j * FBLK:(j + 1) * FBLK],
                    in_=x1_16[b * FBLK:(b + 1) * FBLK, j * 128:(j + 1) * 128],
                    transpose=True), 16)
        blk.sync(xtr)

        amm_base = S("f_amm").n
        asi_base = S("f_asi").n
        afu_base = S("f_afu").n
        gu_base = [s.n for s in s_gu]

        def wl_all(gp, b=b):
            for ic in range(64):
                wp = ic % 2
                n = gic[0] + ic
                if n >= 2:
                    s_amm.wait_at(gp, n - 1)
                s_gu[wp].inc(gp.dma_start(out=wgb[wp][:, :],
                                          in_=I["wgate"][ic, :, :]), 16)
                s_gu[wp].inc(gp.dma_start(out=wub[wp][:, :],
                                          in_=I["wup"][ic, :, :]), 16)
        blk.gpsimd(wl_all)

        def amm_all(te, b=b):
            for ic in range(64):
                wp = ic % 2
                n = gic[0] + ic
                s_gu[wp].wait_at(te, gu_base[wp] + 32 * (ic // 2 + 1))
                s_fw.wait(te)
                if n >= 2:
                    s_afu.wait_at(te, afu_base + ic - 1)
                s_dmm.wait(te)
                for dc in range(16):
                    te.matmul(PG[wp][:, :], wgb[wp][:, dc * 128:(dc + 1) * 128],
                              xfm[:, dc * FBLK:(dc + 1) * FBLK],
                              start=(dc == 0), stop=(dc == 15))
                for dc in range(16):
                    ins = te.matmul(PU[wp][:, :],
                                    wub[wp][:, dc * 128:(dc + 1) * 128],
                                    xfm[:, dc * FBLK:(dc + 1) * FBLK],
                                    start=(dc == 0), stop=(dc == 15))
                s_amm.inc(ins)
        blk.tensor(amm_all)

        def asi_all(se, b=b):
            for ic in range(64):
                wp = ic % 2
                s_amm.wait_at(se, amm_base + ic + 1)
                s_asi.inc(se.activation(sgl[wp][:, :], PG[wp][:, :], AF.Silu,
                                        scale=sgsb[:, ic:ic + 1]))
        blk.scalar(asi_all)

        def afu_all(ve, b=b):
            for ic in range(64):
                wp = ic % 2
                s_asi.wait_at(ve, asi_base + ic + 1)
                s_afu.inc(ve.scalar_tensor_tensor(
                    fuse[:, ic * FBLK:(ic + 1) * FBLK], PU[wp][:, :],
                    susb[:, ic:ic + 1], sgl[wp][:, :], ALU.mult, ALU.mult))
        blk.vector(afu_all)
        gic[0] += 64

        for cg in range(4):
            for icg in range(4):
                wp = gicg[0] % 2
                m = gicg[0]

                def wdl(gp, cg=cg, icg=icg, wp=wp, m=m):
                    if m >= 2:
                        s_dmm.wait_at(gp, m - 1)
                    s_wd[wp].inc(gp.dma_start(
                        out=wdb[wp][:, :].rearrange("p (a t) -> p a t", a=16),
                        in_=I["wdown"][icg * 16:(icg + 1) * 16, :,
                                       cg * 512:(cg + 1) * 512].rearrange(
                                           "a p t -> p a t")), 16)
                blk.gpsimd(wdl)

                def bmm(te, cg=cg, icg=icg, wp=wp):
                    s_wd[wp].wait(te)
                    s_afu.wait(te)
                    s_sd2.wait(te)
                    if icg == 0:
                        s_bev.wait(te)
                    for ic in range(16):
                        for sub in range(4):
                            ins = te.matmul(
                                PD[sub][:, :],
                                fuse[:, (icg * 16 + ic) * FBLK + sub * 128:
                                     (icg * 16 + ic) * FBLK + (sub + 1) * 128],
                                wdb[wp][:, ic * 512:(ic + 1) * 512],
                                start=(icg == 0 and ic == 0),
                                stop=(icg == 3 and ic == 15))
                    s_dmm.inc(ins)
                blk.tensor(bmm)
                gicg[0] += 1

            def bev(ve, b=b, cg=cg):
                s_dmm.wait(ve)
                s_sd2.wait(ve)
                if b > 0 or cg > 0:
                    s_ln5.wait(ve)
                for sub in range(4):
                    ins = ve.tensor_mul(z2[sub][:, cg * 512:(cg + 1) * 512],
                                        PD[sub][:, :],
                                        sdbc[:, cg * 512:(cg + 1) * 512])
                s_bev.inc(ins)
            blk.vector(bev)

        for sub in range(4):
            def x1l(sy, b=b, sub=sub):
                s_flna.wait(sy)
                s_x1r.inc(sy.dma_start(
                    out=x1r[:, :],
                    in_=x1_32[b * FBLK + sub * 128:b * FBLK + (sub + 1) * 128, :]),
                    16)
            blk.sync(x1l)

            def fln(ve, sub=sub):
                s_bev.wait(ve)
                s_x1r.wait(ve)
                hop(ve, ve.tensor_add(z2[sub][:, :], z2[sub][:, :],
                                      x1r[:, :]), "hopV")
                ins = ve.reduce_sum(
                    msum[:, :],
                    z2[sub][:, :].rearrange("p (o d) -> p o d", o=1),
                    axis=AX.X)
                s_flna.inc(ins)
            blk.vector(fln)

            jg = b * 4 + sub

            def fln2(se, sub=sub, nfl=s_flna.n):
                S("c_cc").wait(se)
                s_flna.wait_at(se, nfl)
                se.activation(sqs[:, :], z2[sub][:, :], AF.Square,
                              accum_out=ssq[:, :])
                hop = S("sec")
                hop.inc(se.activation(nmean[:, :], msum[:, :], AF.Copy,
                                      scale=P["cc"][:, 5:6]))
                hop.wait(se)
                s_l2a.inc(se.activation(em2[:, :], nmean[:, :], AF.Square))
            fn_seb.append(fln2)

            def fln3(ve, sub=sub, jg=jg):
                s_l2a.wait_at(ve, jg + 1)
                s_l2v.inc(ve.scalar_tensor_tensor(
                    vtmp[:, :], ssq[:, :], P["cc"][:, 6:7], em2[:, :],
                    ALU.mult, ALU.subtract))
            blk.vector(fln3)

            def fln4(se, sub=sub, jg=jg):
                s_l2v.wait_at(se, jg + 1)
                hop = S("sec")
                hop.inc(se.activation(vtmp[:, :], vtmp[:, :], AF.Ln,
                                      bias=P["cc"][:, 2:3]))
                hop.wait(se)
                s_l2b.inc(se.activation(rstd[:, :], vtmp[:, :], AF.Exp,
                                        scale=P["cc"][:, 3:4]))
            fn_seb.append(fln4)

            def fln5(ve, sub=sub, jg=jg):
                s_l2b.wait_at(ve, jg + 1)
                s_out.wait(ve)
                s_ln5.inc(ve.tensor_scalar(
                    x2b[:, :], z2[sub][:, :], nmean[:, 0:1], rstd[:, 0:1],
                    ALU.add, ALU.mult))
            blk.vector(fln5)

            def fst(sy, b=b, sub=sub):
                s_ln5.wait(sy)
                r0 = b * FBLK + sub * 128
                s_out.inc(sy.dma_start(out=out_t[r0:r0 + 128, :],
                                       in_=x2b[:, :]), 16)
            blk.sync(fst)
        fn_flush()

    def fin(sy):
        s_out.wait(sy)
    blk.sync(fin)
    ps.close()


def kernel(**inputs):
    nc = build()
    in_maps = [core_inputs(inputs, c) for c in range(8)]
    res = run_bass_kernel_spmd(nc, in_maps, core_ids=list(range(8)))
    full = np.empty((4, T, D), np.float32)
    for b in range(4):
        full[b, 0:MYT] = res.results[2 * b]["out"]
        full[b, MYT:T] = res.results[2 * b + 1]["out"]
    return full
